# version 11
# baseline (speedup 1.0000x reference)
import numpy as np
import concourse.bass as bass
import concourse.mybir as mybir
from concourse.bass_utils import run_bass_kernel_spmd

F32 = mybir.dt.float32
ALU = mybir.AluOpType
AF = mybir.ActivationFunctionType
AX = mybir.AxisListType.X

T = 2048
C = 1024
NT = T // 128
NC = C // 128
W = 512
NWIN = 4
STRIDE = 256
E = 8
HID = 2048
NHC = HID // 128


class Sem:
    def __init__(self, h):
        self.h = h
        self.n = 0


class Plan:
    def __init__(self):
        self.q = {k: [] for k in ("sync", "tensor", "vector", "scalar", "gpsimd")}

    def emit(self, eng, fn):
        self.q[eng].append(fn)

    def wait(self, eng, sem, val=None):
        v = sem.n if val is None else val
        assert v <= sem.n, f"wait for future value {v} > {sem.n}"
        if v > 0:
            self.emit(eng, lambda e, s=sem.h, v=v: e.wait_ge(s, v))

    def dma(self, eng, out, in_, sem):
        sem.n += 16
        self.emit(eng, lambda e, o=out, i=in_, s=sem.h: e.dma_start(out=o, in_=i).then_inc(s, 16))

    def op(self, eng, fn, sem=None):
        if sem is not None:
            sem.n += 1
            self.emit(eng, lambda e, f=fn, s=sem.h: f(e).then_inc(s, 1))
        else:
            self.emit(eng, lambda e, f=fn: f(e))


def build_program(debug=False, debug_parts=None, n_experts=E, do_scatter=True, serial_dma=False):
    nc = bass.Bass("TRN2", target_bir_lowering=False, detect_race_conditions=False)

    x_h = nc.dram_tensor("x", [T, C], F32, kind="ExternalInput").ap()
    wqkv_h = nc.dram_tensor("wqkv", [C, 3 * C], F32, kind="ExternalInput").ap()
    bqkv_h = nc.dram_tensor("bqkv", [128, 24], F32, kind="ExternalInput").ap()
    bvrow_h = nc.dram_tensor("bvrow", [8, C], F32, kind="ExternalInput").ap()
    wout_h = nc.dram_tensor("wout", [C, C], F32, kind="ExternalInput").ap()
    bout_h = nc.dram_tensor("bout", [128, NC], F32, kind="ExternalInput").ap()
    w12t_h = nc.dram_tensor("w12t", [E, NHC, 128, 2 * C], F32, kind="ExternalInput").ap()
    w3t_h = nc.dram_tensor("w3t", [E, NC, 128, NHC * 128], F32, kind="ExternalInput").ap()
    gwt_h = nc.dram_tensor("gwt", [C, E], F32, kind="ExternalInput").ap()
    eye_h = nc.dram_tensor("eye", [128, 128], F32, kind="ExternalInput").ap()
    ramp_h = nc.dram_tensor("ramp", [128, E], F32, kind="ExternalInput").ap()
    sel_h = nc.dram_tensor("sel", [8, 8 * 128], F32, kind="ExternalInput").ap()
    I32 = mybir.dt.int32
    tvals_h = nc.dram_tensor("tvals", [128, 16], I32, kind="ExternalInput").ap()
    sut_h = nc.dram_tensor("sut", [128, 128], F32, kind="ExternalInput").ap()
    onesm_h = nc.dram_tensor("onesm", [128, 128], F32, kind="ExternalInput").ap()
    xn_d = nc.dram_tensor("xn_scratch", [T, C], F32, kind="Internal").ap()
    slots_d = nc.dram_tensor("slots_scratch", [6144, 4], I32, kind="Internal").ap()
    out_h = nc.dram_tensor("out", [T, C], F32, kind="ExternalOutput").ap()
    dbg = {}
    if debug_parts is None:
        debug_parts = {"xn_fm", "p_fm", "y_attn", "qkv", "prob", "pt", "opre", "qkv3", "stat"} if debug else set()
    debug = bool(debug_parts)
    if debug:
        dbg["xn_fm"] = nc.dram_tensor("dbg_xn_fm", [128, NC * T], F32, kind="ExternalOutput").ap()
        dbg["p_fm"] = nc.dram_tensor("dbg_p_fm", [8, T], F32, kind="ExternalOutput").ap()
        dbg["y_attn"] = nc.dram_tensor("dbg_y_attn", [128, NC * T], F32, kind="ExternalOutput").ap()
        dbg["qkv"] = nc.dram_tensor("dbg_qkv", [128, 3 * W], F32, kind="ExternalOutput").ap()
        dbg["prob"] = nc.dram_tensor("dbg_prob", [128, W], F32, kind="ExternalOutput").ap()
        dbg["pt"] = nc.dram_tensor("dbg_pt", [128, 4 * W], F32, kind="ExternalOutput").ap()
        dbg["opre"] = nc.dram_tensor("dbg_opre", [64, 1024], F32, kind="ExternalOutput").ap()
        dbg["qkv3"] = nc.dram_tensor("dbg_qkv3", [128, 3 * W], F32, kind="ExternalOutput").ap()
        dbg["stat"] = nc.dram_tensor("dbg_stat", [128, 224], F32, kind="ExternalOutput").ap()
        dbg["xg"] = nc.dram_tensor("dbg_xg", [128, 4096], F32, kind="ExternalOutput").ap()
        dbg["xgtm"] = nc.dram_tensor("dbg_xgtm", [128, 2048], F32, kind="ExternalOutput").ap()
        dbg["eo"] = nc.dram_tensor("dbg_eo", [128, 4096], F32, kind="ExternalOutput").ap()
        dbg["hid"] = nc.dram_tensor("dbg_hid", [128, 8192], F32, kind="ExternalOutput").ap()
        dbg["w12l"] = nc.dram_tensor("dbg_w12l", [128, 4096], F32, kind="ExternalOutput").ap()

    p = Plan()

    from contextlib import ExitStack
    with ExitStack() as _stack:
        _ec = _stack.enter_context
        xn_fm = _ec(nc.sbuf_tensor("xn_fm", [128, NC * T], F32))
        y_fm = _ec(nc.sbuf_tensor("y_fm", [128, NC * T], F32))
        scr = _ec(nc.sbuf_tensor("scr", [128, 16896], F32))
        bqkv_sb = _ec(nc.sbuf_tensor("bqkv_sb", [128, 24], F32))
        gwt_sb = _ec(nc.sbuf_tensor("gwt_sb", [128, NC * E], F32))
        p_fm = _ec(nc.sbuf_tensor("p_fm", [8, T], F32))
        stat = _ec(nc.sbuf_tensor("stat", [128, 224], F32))
        eye_sb = _ec(nc.sbuf_tensor("eye_sb", [128, 128], F32))
        ramp_sb = _ec(nc.sbuf_tensor("ramp_sb", [128, E], F32))
        sel_sb = _ec(nc.sbuf_tensor("sel_sb", [8, 8 * 128], F32))
        tvals_sb = _ec(nc.sbuf_tensor("tvals_sb", [128, 16], I32))
        bout_sb = _ec(nc.sbuf_tensor("bout_sb", [128, NC], F32))
        xt0 = scr[:, 12800:13824]
        xt1 = scr[:, 13824:14848]
        ob0 = scr[:, 14848:15872]
        ob1 = scr[:, 15872:16896]
        sqjunk = scr[:, 14848:15872]
        xnt0 = scr[:, 14848:15872]
        xnt1 = scr[:, 15872:16896]
        sut_sb = scr[:, 3072:3200]
        onesm_sb = scr[:, 3200:3328]
        quads_sb = scr[:, 3328:3456].bitcast(mybir.dt.int32)
        sidxf_sb = scr[:, 3456:3536]
        sidx_sb = scr[:, 3536:3584].bitcast(mybir.dt.int32)
        scg_sb = scr[:, 0:NT * E]
        pw_sb = scr[:, 256:256 + NT * E]
        z1b = scr[:, 1024:1152]
        one1b = scr[:, 1152:1280]
        sc2b = scr[:, 1280:1408]
        z2b = scr[:, 1408:1536]
        one2b = scr[:, 1536:1664]
        wq_sl0 = scr[:, 0:1024]
        wq_sl1 = scr[:, 1024:2048]
        wk_sl0 = scr[:, 2048:3072]
        wk_sl1 = scr[:, 3072:4096]
        wv_sl0 = scr[:, 4096:5120]
        wv_sl1 = scr[:, 5120:6144]
        wo_sl0 = scr[0:64, 6144:8192]
        q_sb = scr[:, 8192:8704]
        k_sb = scr[:, 8704:9216]
        v_sb = scr[:, 9216:9728]
        opre = scr[0:64, 9728:10752]
        prob = scr[:, 10752:11264]
        pt_sb = scr[:, 11264:13312]
        bv_sb = scr[0:8, 15872:16896]
        prob_b = [scr[:, 10752:11264], scr[:, 13312:13824]]
        pt_b = [scr[:, 11264:13312], scr[:, 13824:15872]]
        hid_sb = scr[:, 0:NHC * W]
        w12_sl0 = scr[:, 8192:10240]
        w12_sl1 = scr[:, 10240:12288]
        w3_sl0 = scr[:, 12288:14336]
        w3_sl1 = scr[:, 14336:16384]
        pb_sb = scr[:, 16384:16896]
        psA = _ec(nc.psum_tensor("psA", [128, 512], F32))
        psB = _ec(nc.psum_tensor("psB", [128, 512], F32))
        psC = _ec(nc.psum_tensor("psC", [128, 512], F32))
        psD = _ec(nc.psum_tensor("psD", [128, 512], F32))
        psE = _ec(nc.psum_tensor("psE", [128, 512], F32))
        psT = _ec(nc.psum_tensor("psT", [128, 1024], F32))
        psG = _ec(nc.psum_tensor("psG", [128, 512], F32))
        _s_in = _ec(nc.semaphore("s_in"))
        _s_w0 = _ec(nc.semaphore("s_w0"))
        _s_w1 = _ec(nc.semaphore("s_w1"))
        _s_t = _ec(nc.semaphore("s_t"))
        _s_v = _ec(nc.semaphore("s_v"))
        _s_s = _ec(nc.semaphore("s_s"))
        _s_st = _ec(nc.semaphore("s_st"))
        _s_g = _ec(nc.semaphore("s_g"))
        _s_a0 = _ec(nc.semaphore("s_a0"))
        _s_a1 = _ec(nc.semaphore("s_a1"))
        block = _ec(nc.Block())
        s_in = Sem(_s_in)
        s_w0 = Sem(_s_w0)
        s_w1 = Sem(_s_w1)
        s_t = Sem(_s_t)
        s_v = Sem(_s_v)
        s_s = Sem(_s_s)
        s_st = Sem(_s_st)
        s_g = Sem(_s_g)
        s_a0 = Sem(_s_a0)
        s_a1 = Sem(_s_a1)
        slots_sb = scr[:, 8192:8384].bitcast(mybir.dt.int32)
        wsem = [s_w0, s_w1]
        asem = [s_a0, s_a1]

        def vgap():
            p.op("vector", lambda e: e.tensor_copy(stat[:, 220:221], stat[:, 221:222]))

        def sgap():
            p.op("scalar", lambda e: e.copy(stat[:, 222:223], stat[:, 221:222]))

        def vgap2():
            p.op("vector", lambda e: e.tensor_copy(scr[:, 15104:15360], scr[:, 14848:15104]))
            p.op("vector", lambda e: e.tensor_copy(scr[:, 15104:15360], scr[:, 14848:15104]))

        S_SUM, S_SS, S_MEAN, S_VAR, S_VPE, S_SD, S_R0, S_U, S_RSTD, S_NMR = (
            0, 20, 40, 60, 80, 100, 120, 140, 160, 180)

        def full_barrier():
            p.op("vector", lambda e: e.tensor_copy(stat[:, 220:221], stat[:, 221:222]), s_v)
            p.op("scalar", lambda e: e.copy(stat[:, 222:223], stat[:, 221:222]), s_s)
            for eng in ("vector", "scalar", "tensor", "sync", "gpsimd"):
                for sem in (s_v, s_s, s_t, s_in, s_w0, s_w1, s_a0, s_a1, s_st):
                    p.wait(eng, sem, sem.n)

        p.op("vector", lambda e: e.memset(stat[:], 0.0))
        p.dma("sync", eye_sb[:], eye_h, s_in)
        p.dma("sync", ramp_sb[:], ramp_h, s_in)
        p.dma("sync", sel_sb[:], sel_h, s_in)
        p.dma("sync", tvals_sb[:], tvals_h, s_in)
        p.dma("sync", sut_sb[:], sut_h, s_in)
        p.dma("sync", onesm_sb[:], onesm_h, s_in)
        p.dma("sync", bqkv_sb[:], bqkv_h, s_in)
        p.dma("sync", bout_sb[:], bout_h, s_in)
        p.dma("sync", gwt_sb[:].rearrange("p (dc e) -> p dc e", dc=NC),
              gwt_h.rearrange("(dc p) e -> p dc e", p=128), s_in)

        xb = [xt0, xt1]
        xb_free = [None, None]
        for ti in range(NT):
            b = ti % 2
            if xb_free[b] is not None:
                p.wait("sync", *xb_free[b])
            p.dma("sync", xb[b][:], x_h[ti * 128:(ti + 1) * 128, :], s_in)
            ld_at = s_in.n
            p.wait("vector", s_in, ld_at)
            p.op("vector", lambda e, xb=xb[b], ti=ti: e.reduce_sum(
                stat[:, S_SUM + ti:S_SUM + ti + 1], xb[:], axis=AX))
            p.op("vector", lambda e, xb=xb[b]: e.tensor_mul(sqjunk[:], xb[:], xb[:]))
            p.op("vector", lambda e, ti=ti: e.reduce_sum(
                stat[:, S_SS + ti:S_SS + ti + 1], sqjunk[:], axis=AX), s_v)
            xb_free[b] = (s_v, s_v.n)
        vgap2()
        vgap2()
        ST = lambda g: stat[:, g:g + 20]
        p.op("vector", lambda e: e.tensor_scalar_mul(ST(S_MEAN), ST(S_SUM), 1.0 / C))
        vgap2()
        p.op("vector", lambda e: e.tensor_mul(ST(S_U), ST(S_MEAN), ST(S_MEAN)))
        vgap2()
        p.op("vector", lambda e: e.scalar_tensor_tensor(ST(S_VAR), ST(S_SS), 1.0 / C, ST(S_U),
                                                        ALU.mult, ALU.subtract))
        vgap2()
        p.op("vector", lambda e: e.tensor_scalar_add(ST(S_VPE), ST(S_VAR), 1e-5), s_v)
        p.wait("scalar", s_v, s_v.n)
        sgap()
        sgap()
        p.op("scalar", lambda e: e.activation(ST(S_SD), ST(S_VPE), AF.Sqrt), s_s)
        p.wait("vector", s_s, s_s.n)
        vgap2()
        p.op("vector", lambda e: e.reciprocal(ST(S_R0), ST(S_SD)))
        vgap2()
        p.op("vector", lambda e: e.tensor_mul(ST(S_U), ST(S_R0), ST(S_R0)))
        vgap2()
        p.op("vector", lambda e: e.tensor_mul(ST(S_U), ST(S_U), ST(S_VPE)))
        vgap2()
        p.op("vector", lambda e: e.tensor_scalar(ST(S_U), ST(S_U), -0.5, 1.5, ALU.mult, ALU.add))
        vgap2()
        p.op("vector", lambda e: e.tensor_mul(ST(S_RSTD), ST(S_R0), ST(S_U)))
        vgap2()
        p.op("vector", lambda e: e.tensor_mul(ST(S_NMR), ST(S_MEAN), ST(S_RSTD)))
        vgap2()
        p.op("vector", lambda e: e.tensor_scalar_mul(ST(S_NMR), ST(S_NMR), -1.0))
        vgap2()
        p.op("vector", lambda e: e.tensor_copy(stat[:, 220:221], stat[:, 221:222]), s_v)
        p.wait("sync", s_v, s_v.n)
        p.dma("sync", scr[:, 4096:4320], stat[:], s_in)
        stats_flush = s_in.n
        stats_at = s_v.n
        if "stat" in debug_parts:
            p.wait("sync", s_v, stats_at)
            p.dma("sync", dbg["stat"], stat[:], s_st)
            p.wait("vector", s_st, s_st.n)
            p.wait("tensor", s_st, s_st.n)
            p.wait("scalar", s_st, s_st.n)
            p.wait("gpsimd", s_st, s_st.n)
            p.wait("sync", s_st, s_st.n)
            p.wait("vector", s_st, s_st.n)
            p.wait("tensor", s_st, s_st.n)
            p.wait("sync", s_st, s_st.n)

        xnb = [xnt0, xnt1]
        xnb_free = [None, None]
        xb_free = [None, None]
        for ti in range(NT):
            b = ti % 2
            if xb_free[b] is not None:
                p.wait("sync", *xb_free[b])
            p.dma("sync", xb[b][:], x_h[ti * 128:(ti + 1) * 128, :], s_in)
            ld_at = s_in.n
            p.wait("vector", s_in, ld_at)
            if ti == 0:
                p.wait("vector", s_v, stats_at)
                p.wait("vector", s_in, stats_flush)
            if xnb_free[b] is not None:
                p.wait("vector", *xnb_free[b])
                p.wait("vector", s_in, s_in.n)
            p.op("vector", lambda e, xb=xb[b], xn=xnb[b], ti=ti: e.tensor_scalar(
                xn[:], xb[:], stat[:, S_RSTD + ti:S_RSTD + ti + 1],
                stat[:, S_NMR + ti:S_NMR + ti + 1], ALU.mult, ALU.add), s_v)
            xb_free[b] = (s_v, s_v.n)
            xnt_at = s_v.n
            p.wait("sync", s_v, xnt_at)
            p.dma("sync", xn_d[ti * 128:(ti + 1) * 128, :], xnb[b][:], s_in)
            xn_st_at = s_in.n
            xnb_free2 = (s_in, xn_st_at)
            p.wait("tensor", s_v, xnt_at)
            for cc in range(NC):
                p.op("tensor", lambda e, cc=cc, xn=xnb[b]: e.transpose(
                    psT[:, cc * 128:(cc + 1) * 128], xn[:, cc * 128:(cc + 1) * 128], eye_sb[:]),
                    s_t if cc == NC - 1 else None)
            tp_at = s_t.n
            xnb_free[b] = (s_t, tp_at)
            xnb_hbm_free = xnb_free2
            p.wait("vector", s_t, tp_at)
            p.op("vector", lambda e, ti=ti: e.tensor_copy(
                xn_fm[:].rearrange("p (cc t) -> p cc t", cc=NC)[:, :, ti * 128:(ti + 1) * 128],
                psT[:].rearrange("p (cc t) -> p cc t", cc=NC)), s_v)
            evac_at = s_v.n
            p.wait("tensor", s_v, evac_at)
            for dc in range(NC):
                p.op("tensor", lambda e, dc=dc, ti=ti: e.matmul(
                    psG[:, ti * 8: ti * 8 + 8],
                    xn_fm[:, dc * T + ti * 128: dc * T + ti * 128 + 128],
                    gwt_sb[:, dc * E:(dc + 1) * E],
                    start=(ti == 0 and dc == 0), stop=(ti == NT - 1 and dc == NC - 1)),
                    s_t if (dc == NC - 1) else None)
        gate_done = s_t.n

        if "xn_fm" in debug_parts:
            p.wait("sync", s_v, s_v.n)
            p.dma("sync", dbg["xn_fm"], xn_fm[:], s_st)
            p.wait("vector", s_st, s_st.n)
            p.wait("tensor", s_st, s_st.n)
            p.wait("scalar", s_st, s_st.n)
            p.wait("gpsimd", s_st, s_st.n)
            p.wait("sync", s_st, s_st.n)
            p.wait("vector", s_st, s_st.n)
            p.wait("tensor", s_st, s_st.n)
            p.wait("sync", s_st, s_st.n)

        M1, R1, M2, R2, D12, P1, P2 = 0, 20, 40, 60, 80, 100, 120
        p.wait("vector", s_t, gate_done)
        p.op("vector", lambda e: e.tensor_copy(scg_sb[:], psG[:, 0:NT * E]), s_v)
        vgap()
        for ti in range(NT):
            p.op("vector", lambda e, ti=ti: e.reduce_max(
                stat[:, M1 + ti:M1 + ti + 1], scg_sb[:, ti * E:(ti + 1) * E], axis=AX))
        for ti in range(NT):
            p.op("vector", lambda e, ti=ti: e.tensor_scalar(
                z1b[:, ti * E:(ti + 1) * E], scg_sb[:, ti * E:(ti + 1) * E],
                stat[:, M1 + ti:M1 + ti + 1], None, ALU.is_equal))
        for ti in range(NT):
            p.op("vector", lambda e, ti=ti: e.tensor_mul(
                z1b[:, ti * E:(ti + 1) * E], z1b[:, ti * E:(ti + 1) * E], ramp_sb[:]))
        for ti in range(NT):
            p.op("vector", lambda e, ti=ti: e.reduce_max(
                stat[:, R1 + ti:R1 + ti + 1], z1b[:, ti * E:(ti + 1) * E], axis=AX))
        for ti in range(NT):
            p.op("vector", lambda e, ti=ti: e.tensor_scalar(
                one1b[:, ti * E:(ti + 1) * E], ramp_sb[:],
                stat[:, R1 + ti:R1 + ti + 1], None, ALU.is_equal))
        for ti in range(NT):
            p.op("vector", lambda e, ti=ti: e.scalar_tensor_tensor(
                sc2b[:, ti * E:(ti + 1) * E], one1b[:, ti * E:(ti + 1) * E], -1e30,
                scg_sb[:, ti * E:(ti + 1) * E], ALU.mult, ALU.add))
        for ti in range(NT):
            p.op("vector", lambda e, ti=ti: e.reduce_max(
                stat[:, M2 + ti:M2 + ti + 1], sc2b[:, ti * E:(ti + 1) * E], axis=AX))
        for ti in range(NT):
            p.op("vector", lambda e, ti=ti: e.tensor_scalar(
                z2b[:, ti * E:(ti + 1) * E], sc2b[:, ti * E:(ti + 1) * E],
                stat[:, M2 + ti:M2 + ti + 1], None, ALU.is_equal))
        for ti in range(NT):
            p.op("vector", lambda e, ti=ti: e.tensor_mul(
                z2b[:, ti * E:(ti + 1) * E], z2b[:, ti * E:(ti + 1) * E], ramp_sb[:]))
        for ti in range(NT):
            p.op("vector", lambda e, ti=ti: e.reduce_max(
                stat[:, R2 + ti:R2 + ti + 1], z2b[:, ti * E:(ti + 1) * E], axis=AX))
        for ti in range(NT):
            p.op("vector", lambda e, ti=ti: e.tensor_scalar(
                one2b[:, ti * E:(ti + 1) * E], ramp_sb[:],
                stat[:, R2 + ti:R2 + ti + 1], None, ALU.is_equal))
        p.op("vector", lambda e: e.tensor_sub(stat[:, D12:D12 + 20], stat[:, M1:M1 + 20],
                                              stat[:, M2:M2 + 20]), s_v)
        p.wait("scalar", s_v, s_v.n)
        sgap()
        sgap()
        p.op("scalar", lambda e: e.activation(stat[:, P1:P1 + 20], stat[:, D12:D12 + 20],
                                              AF.Sigmoid), s_s)
        p.wait("vector", s_s, s_s.n)
        vgap()
        p.op("vector", lambda e: e.tensor_scalar(stat[:, P2:P2 + 20], stat[:, P1:P1 + 20],
                                                 -1.0, 1.0, ALU.mult, ALU.add))
        vgap()
        for ti in range(NT):
            p.op("vector", lambda e, ti=ti: e.tensor_scalar_mul(
                pw_sb[:, ti * E:(ti + 1) * E], one1b[:, ti * E:(ti + 1) * E],
                stat[:, P1 + ti:P1 + ti + 1]))
        for ti in range(NT):
            p.op("vector", lambda e, ti=ti: e.scalar_tensor_tensor(
                pw_sb[:, ti * E:(ti + 1) * E], one2b[:, ti * E:(ti + 1) * E],
                stat[:, P2 + ti:P2 + ti + 1], pw_sb[:, ti * E:(ti + 1) * E],
                ALU.mult, ALU.add), None)
        CAP = 768
        mb = scr[:, 1664:1792]
        rb = scr[:, 1792:1920]
        tm8 = scr[:, 1920:2048]
        RANK1, RANK2 = 140, 160
        E1g, E2g, G1g, G2g, RR1, RR2 = 0, 40, 20, 60, 180, 200
        for ti in range(NT):
            p.op("vector", lambda e, ti=ti: e.tensor_add(
                mb[:, ti * E:(ti + 1) * E], one1b[:, ti * E:(ti + 1) * E],
                one2b[:, ti * E:(ti + 1) * E]), None)
        p.op("vector", lambda e: e.tensor_copy(stat[:, 220:221], stat[:, 221:222]), s_v)
        pw_done = s_v.n
        p.wait("sync", s_v, pw_done)
        p.dma("sync", scr[:, 5632:5760], scr[:, 256:384], s_in)
        p.dma("sync", scr[:, 5760:6528], scr[:, 1024:1792], s_in)
        pw_flush = s_in.n
        p.wait("tensor", s_in, pw_flush)
        first = True
        for i in range(NT):
            for j in range(i + 1):
                lhs = sut_sb if j == i else onesm_sb
                p.op("tensor", lambda e, i=i, j=j, lhs=lhs, st=first: e.matmul(
                    psE[:, i * 8:(i + 1) * 8], lhs[:], mb[:, j * E:(j + 1) * E],
                    start=st, stop=(i == NT - 1 and j == i)),
                    s_t if (i == NT - 1 and j == i) else None)
                first = False
        rmm_at = s_t.n
        p.wait("vector", s_t, rmm_at)
        p.op("vector", lambda e: e.tensor_copy(rb[:], psE[:, 0:128]), s_v)
        vgap()
        for ti in range(NT):
            p.op("vector", lambda e, ti=ti: e.tensor_mul(
                tm8[:, ti * E:(ti + 1) * E], rb[:, ti * E:(ti + 1) * E],
                one1b[:, ti * E:(ti + 1) * E]))
        for ti in range(NT):
            p.op("vector", lambda e, ti=ti: e.reduce_sum(
                stat[:, RANK1 + ti:RANK1 + ti + 1], tm8[:, ti * E:(ti + 1) * E], axis=AX))
        for ti in range(NT):
            p.op("vector", lambda e, ti=ti: e.tensor_mul(
                tm8[:, ti * E:(ti + 1) * E], rb[:, ti * E:(ti + 1) * E],
                one2b[:, ti * E:(ti + 1) * E]))
        for ti in range(NT):
            p.op("vector", lambda e, ti=ti: e.reduce_sum(
                stat[:, RANK2 + ti:RANK2 + ti + 1], tm8[:, ti * E:(ti + 1) * E], axis=AX))
        vgap2()
        p.op("vector", lambda e: e.tensor_scalar(stat[:, E1g:E1g + 20], stat[:, R1:R1 + 20],
                                                 -1.0, 8.0, ALU.mult, ALU.add))
        p.op("vector", lambda e: e.tensor_scalar(stat[:, E2g:E2g + 20], stat[:, R2:R2 + 20],
                                                 -1.0, 8.0, ALU.mult, ALU.add))
        vgap2()
        scrA = sidxf_sb[:, 40:60]
        scrB = sidxf_sb[:, 60:80]
        for kk, (RK, Gg) in enumerate(((RANK1, G1g), (RANK2, G2g))):
            p.op("vector", lambda e, RK=RK, Gg=Gg: e.tensor_scalar(
                stat[:, Gg:Gg + 20], stat[:, RK:RK + 20], 128.0, None, ALU.is_ge))
        vgap2()
        for m in range(2, 6):
            for kk, (RK, Gg) in enumerate(((RANK1, G1g), (RANK2, G2g))):
                p.op("vector", lambda e, RK=RK, m=m: e.tensor_scalar(
                    scrA, stat[:, RK:RK + 20], float(m * 128), None, ALU.is_ge))
                vgap2()
                p.op("vector", lambda e, Gg=Gg: e.tensor_add(
                    stat[:, Gg:Gg + 20], stat[:, Gg:Gg + 20], scrA))
                vgap2()
        for kk, (RK, Gg, RRg) in enumerate(((RANK1, G1g, RR1), (RANK2, G2g, RR2))):
            p.op("vector", lambda e, RK=RK, Gg=Gg, RRg=RRg: e.scalar_tensor_tensor(
                stat[:, RRg:RRg + 20], stat[:, Gg:Gg + 20], -128.0, stat[:, RK:RK + 20],
                ALU.mult, ALU.add))
        vgap2()
        for kk, (Eg, Gg, RRg, o) in enumerate(((E1g, G1g, RR1, 0), (E2g, G2g, RR2, 16))):
            p.op("vector", lambda e, Eg=Eg, Gg=Gg: e.scalar_tensor_tensor(
                scrB[:, 0:16], stat[:, Eg:Eg + 16], 6.0, stat[:, Gg:Gg + 16],
                ALU.mult, ALU.add))
            vgap2()
            p.op("vector", lambda e, RRg=RRg, o=o: e.scalar_tensor_tensor(
                sidxf_sb[:, o:o + 16], stat[:, RRg:RRg + 16], 48.0,
                scrB[:, 0:16], ALU.mult, ALU.add))
            vgap2()
        p.op("vector", lambda e: e.memset(quads_sb[:], 0))
        qv = quads_sb[:].rearrange("p (g q) -> p g q", q=4)
        for k in range(2):
            Pg = P1 if k == 0 else P2
            p.op("vector", lambda e, k=k, qv=qv: e.tensor_copy(
                qv[:, k * 16:(k + 1) * 16, 0:1], tvals_sb[:].rearrange("p (t o) -> p t o", o=1)))
            p.op("vector", lambda e, k=k, qv=qv, Pg=Pg: e.tensor_copy(
                qv[:, k * 16:(k + 1) * 16, 1:2].bitcast(F32),
                stat[:, Pg:Pg + 16].rearrange("p (t o) -> p t o", o=1)))
        p.op("vector", lambda e: e.memset(slots_sb[:], 0), s_v)
        quads_at = s_v.n
        p.wait("sync", s_v, quads_at)
        p.dma("sync", slots_d.rearrange("(pp g) q -> pp (g q)", g=48), slots_sb[:], s_in)
        slots_zero = s_in.n
        p.dma("sync", scr[:, 7424:7456], sidxf_sb[:, 0:32], s_in)
        sidx_flush = s_in.n
        p.wait("gpsimd", s_in, sidx_flush)
        p.op("gpsimd", lambda e: e.tensor_copy(sidx_sb[:, 0:32], sidxf_sb[:, 0:32]), s_s)
        sidx_conv = s_s.n
        p.wait("gpsimd", s_in, slots_zero)
        from concourse.bass import IndirectOffsetOnAxis as IOA
        for k in range(2):
            for ti in range(NT):
                col = k * 16 + ti
                s_g.n += 16
                p.emit("gpsimd", lambda e, col=col, qv=qv: e.indirect_dma_start(
                    slots_d, IOA(sidx_sb[:, col:col + 1], 0),
                    quads_sb[:, col * 4:(col + 1) * 4], None).then_inc(s_g.h, 16))
        slot_scatters = s_g.n
        pw_done = s_v.n
        p.wait("tensor", s_v, pw_done)
        for rnd in range(2):
            for j in range(8):
                ti = rnd * 8 + j
                p.op("tensor", lambda e, ti=ti, j=j: e.transpose(
                    psT[0:8, j * 128:(j + 1) * 128], pw_sb[:, ti * E:(ti + 1) * E], eye_sb[:]),
                    s_t if j == 7 else None)
            p.wait("vector", s_t, s_t.n)
            p.op("vector", lambda e, rnd=rnd: e.tensor_copy(
                p_fm[:, rnd * 1024:(rnd + 1) * 1024], psT[0:8, :]), s_v)
            if rnd == 0:
                p.wait("tensor", s_v, s_v.n)
        full_barrier()
        routing_done = s_v.n
        if "p_fm" in debug_parts:
            p.wait("sync", s_v, routing_done)
            p.dma("sync", dbg["p_fm"], p_fm[:], s_st)
            p.wait("vector", s_st, s_st.n)
            p.wait("tensor", s_st, s_st.n)
            p.wait("scalar", s_st, s_st.n)
            p.wait("gpsimd", s_st, s_st.n)
            p.wait("sync", s_st, s_st.n)
            p.wait("vector", s_st, s_st.n)
            p.wait("tensor", s_st, s_st.n)
            p.wait("sync", s_st, s_st.n)

        def load_wslice(dst, col0, ncols, sem):
            p.dma("sync", dst[:, 0:NC * ncols].rearrange("p (dc j) -> p dc j", dc=NC),
                  wqkv_h.rearrange("(dc p) m -> p dc m", p=128)[:, :, col0:col0 + ncols], sem)

        wq_b = [wq_sl0, wq_sl1]
        wk_b = [wk_sl0, wk_sl1]
        wv_b = [wv_sl0, wv_sl1]
        wfree = [(s_v, routing_done), (s_v, routing_done)]
        wofree = [None]
        p.wait("sync", s_v, routing_done)
        p.wait("sync", s_g, slot_scatters)
        p.dma("sync", bv_sb[:], bvrow_h, s_in)
        bv_at = s_in.n

        for hp in range(8):
            b = hp % 2
            if wfree[b] is not None:
                p.wait("sync", *wfree[b])
            load_wslice(wq_b[b], hp * 128, 128, wsem[b])
            load_wslice(wk_b[b], C + hp * 128, 128, wsem[b])
            load_wslice(wv_b[b], 2 * C + hp * 128, 128, wsem[b])
            wld_at = wsem[b].n
            if wofree[0] is not None:
                p.wait("sync", *wofree[0])
            p.dma("sync", wo_sl0[:].rearrange("p (hh m) -> p hh m", hh=2),
                  wout_h.rearrange("(g p) m -> g p m", p=64)[2 * hp:2 * hp + 2].rearrange(
                      "g p m -> p g m"), wsem[0])
            wo_at = wsem[0].n
            for win in range(NWIN):
                t0 = win * STRIDE

                p.wait("tensor", wsem[b], wld_at)
                if win == 0 and hp == 0:
                    p.wait("tensor", s_in, bv_at)
                p.wait("tensor", s_v, s_v.n)
                for dc in range(NC):
                    p.op("tensor", lambda e, dc=dc, b=b, t0=t0: e.matmul(
                        psA[:], wq_b[b][:, dc * 128:(dc + 1) * 128],
                        xn_fm[:, dc * T + t0: dc * T + t0 + W],
                        start=(dc == 0), stop=(dc == NC - 1)), s_t if dc == NC - 1 else None)
                qmm_at = s_t.n
                for dc in range(NC):
                    p.op("tensor", lambda e, dc=dc, b=b, t0=t0: e.matmul(
                        psB[:], wk_b[b][:, dc * 128:(dc + 1) * 128],
                        xn_fm[:, dc * T + t0: dc * T + t0 + W],
                        start=(dc == 0), stop=(dc == NC - 1)), s_t if dc == NC - 1 else None)
                kmm_at = s_t.n
                for ts in range(4):
                    for dc in range(NC):
                        p.op("tensor", lambda e, ts=ts, dc=dc, b=b, t0=t0: e.matmul(
                            psE[:, ts * 128:(ts + 1) * 128],
                            xn_fm[:, dc * T + t0 + ts * 128: dc * T + t0 + ts * 128 + 128],
                            wv_b[b][:, dc * 128:(dc + 1) * 128],
                            start=(ts == 0 and dc == 0), stop=False))
                    p.op("tensor", lambda e, ts=ts, hp=hp: e.matmul(
                        psE[:, ts * 128:(ts + 1) * 128], sel_sb[:, 0:128],
                        bv_sb[:, hp * 128:(hp + 1) * 128],
                        start=False, stop=(ts == 3)), s_t if ts == 3 else None)
                vmm_at = s_t.n

                p.wait("vector", s_t, qmm_at)
                p.op("vector", lambda e, hp=hp: e.tensor_scalar_add(
                    q_sb[:], psA[:], bqkv_sb[:, hp:hp + 1]), s_v)
                p.wait("vector", s_t, kmm_at)
                p.op("vector", lambda e, hp=hp: e.tensor_scalar_add(
                    k_sb[:], psB[:], bqkv_sb[:, 8 + hp:9 + hp]), s_v)
                p.wait("vector", s_t, vmm_at)
                p.op("vector", lambda e: e.tensor_copy(v_sb[:], psE[:]), s_v)
                wfree[b] = (s_v, s_v.n)
                qkv_at = s_v.n
                if "qkv3" in debug_parts and win == 3 and hp == 0:
                    p.wait("sync", s_v, qkv_at)
                    p.dma("sync", dbg["qkv3"][:, 0:W], q_sb[:], s_st)
                    p.wait("vector", s_st, s_st.n)
                    p.wait("tensor", s_st, s_st.n)
                    p.wait("scalar", s_st, s_st.n)
                    p.wait("gpsimd", s_st, s_st.n)
                    p.wait("sync", s_st, s_st.n)
                    p.dma("sync", dbg["qkv3"][:, W:2 * W], k_sb[:], s_st)
                    p.dma("sync", dbg["qkv3"][:, 2 * W:3 * W], v_sb[:], s_st)
                    p.wait("vector", s_st, s_st.n)
                    p.wait("tensor", s_st, s_st.n)
                    p.wait("scalar", s_st, s_st.n)
                    p.wait("gpsimd", s_st, s_st.n)
                    p.wait("sync", s_st, s_st.n)
                    p.wait("vector", s_st, s_st.n)
                    p.wait("tensor", s_st, s_st.n)
                    p.wait("sync", s_st, s_st.n)
                if "qkv" in debug_parts and win == 0 and hp == 0:
                    p.wait("sync", s_v, qkv_at)
                    p.dma("sync", dbg["qkv"][:, 0:W], q_sb[:], s_st)
                    p.wait("vector", s_st, s_st.n)
                    p.wait("tensor", s_st, s_st.n)
                    p.wait("scalar", s_st, s_st.n)
                    p.wait("gpsimd", s_st, s_st.n)
                    p.wait("sync", s_st, s_st.n)
                    p.wait("vector", s_st, s_st.n)
                    p.wait("tensor", s_st, s_st.n)
                    p.wait("sync", s_st, s_st.n)
                    p.dma("sync", dbg["qkv"][:, W:2 * W], k_sb[:], s_st)
                    p.wait("vector", s_st, s_st.n)
                    p.wait("tensor", s_st, s_st.n)
                    p.wait("scalar", s_st, s_st.n)
                    p.wait("gpsimd", s_st, s_st.n)
                    p.wait("sync", s_st, s_st.n)
                    p.dma("sync", dbg["qkv"][:, 2 * W:3 * W], v_sb[:], s_st)
                    p.wait("vector", s_st, s_st.n)
                    p.wait("tensor", s_st, s_st.n)
                    p.wait("sync", s_st, s_st.n)

                steps = [(hh, qs) for hh in range(2) for qs in range(4)]
                sc_done = [None] * 8
                sm_done = [None] * 8
                smread = [None] * 8
                tp_done2 = [None] * 8
                ev_done = [None] * 8

                def emit_score(s):
                    hh, qs = steps[s]
                    ps = psC if s % 2 == 0 else psD
                    if s < 2:
                        p.wait("tensor", s_v, qkv_at)
                    else:
                        p.wait("tensor", s_s, smread[s - 2])
                    p.op("tensor", lambda e, ps=ps, pofs=64 * hh, qs=qs: e.matmul(
                        ps[:], q_sb[pofs:pofs + 64, qs * 128:(qs + 1) * 128],
                        k_sb[pofs:pofs + 64, :], start=True, stop=True), s_t)
                    sc_done[s] = s_t.n

                def emit_softmax(s):
                    hh, qs = steps[s]
                    ps = psC if s % 2 == 0 else psD
                    pb = prob_b[s % 2]
                    if s >= 2:
                        p.wait("scalar", s_t, tp_done2[s - 2])
                    p.wait("scalar", s_t, sc_done[s])
                    p.op("scalar", lambda e, ps=ps, pb=pb: e.activation(
                        pb[:], ps[:], AF.Exp, scale=0.125), s_s)
                    smread[s] = s_s.n
                    p.wait("vector", s_s, s_s.n)
                    p.op("vector", lambda e, pb=pb: e.reduce_sum(stat[:, 2:3], pb[:], axis=AX))
                    vgap()
                    p.op("vector", lambda e: e.reciprocal(stat[:, 3:4], stat[:, 2:3]))
                    vgap()
                    p.op("vector", lambda e, pb=pb: e.tensor_scalar_mul(
                        pb[:], pb[:], stat[:, 3:4]), s_v)
                    sm_done[s] = s_v.n

                def emit_transposes(s):
                    hh, qs = steps[s]
                    pb = prob_b[s % 2]
                    p.wait("tensor", s_v, sm_done[s])
                    if s >= 2:
                        p.wait("tensor", s_v, ev_done[s - 2])
                    for ks in range(4):
                        p.op("tensor", lambda e, ks=ks, s=s, pb=pb: e.transpose(
                            psT[:, (s % 2) * 512 + ks * 128:(s % 2) * 512 + (ks + 1) * 128],
                            pb[:, ks * 128:(ks + 1) * 128], eye_sb[:]),
                            s_t if ks == 3 else None)
                    tp_done2[s] = s_t.n
                    p.wait("vector", s_t, tp_done2[s])
                    p.op("vector", lambda e, qs=qs, hh=hh, s=s: e.tensor_copy(
                        pt_b[hh][:].rearrange("p (ks q) -> p ks q", ks=4)[:, :, qs * 128:(qs + 1) * 128],
                        psT[:, (s % 2) * 512:(s % 2) * 512 + 512].rearrange(
                            "p (ks q) -> p ks q", ks=4)), s_v)
                    ev_done[s] = s_v.n

                def emit_av(hh):
                    p.wait("tensor", s_v, ev_done[hh * 4 + 3])
                    for ks in range(4):
                        p.op("tensor", lambda e, ks=ks, hh=hh: e.matmul(
                            psG[0:64, :], v_sb[:, ks * 128 + 64 * hh: ks * 128 + 64 * hh + 64],
                            pt_b[hh][:, ks * W:(ks + 1) * W],
                            start=(ks == 0), stop=(ks == 3)), s_t if ks == 3 else None)
                    p.wait("vector", s_t, s_t.n)
                    p.op("vector", lambda e, hh=hh: e.tensor_copy(
                        opre[:, hh * 512:(hh + 1) * 512], psG[0:64, :]), s_v)

                emit_score(0)
                emit_softmax(0)
                emit_score(1)
                for s in range(8):
                    if s + 1 < 8:
                        emit_softmax(s + 1)
                    if s + 2 < 8:
                        emit_score(s + 2)
                    emit_transposes(s)
                    if s == 3:
                        emit_av(0)
                    if s == 7:
                        emit_av(1)
                ow_at = s_v.n
                if "opre" in debug_parts and win == 0 and hp == 0:
                    p.wait("sync", s_v, ow_at)
                    p.dma("sync", dbg["opre"], opre[:], s_st)
                    p.wait("vector", s_st, s_st.n)
                    p.wait("tensor", s_st, s_st.n)
                    p.wait("scalar", s_st, s_st.n)
                    p.wait("gpsimd", s_st, s_st.n)
                    p.wait("sync", s_st, s_st.n)
                    p.wait("vector", s_st, s_st.n)
                    p.wait("tensor", s_st, s_st.n)
                    p.wait("sync", s_st, s_st.n)

                p.wait("tensor", wsem[0], wo_at)
                p.wait("tensor", s_v, ow_at)
                for cc in range(NC):
                    ps = psA if cc % 2 == 0 else psB
                    p.wait("tensor", s_v, s_v.n)
                    for hh in range(2):
                        p.op("tensor", lambda e, ps=ps, cc=cc, hh=hh: e.matmul(
                            ps[:], wo_sl0[:, hh * 1024 + cc * 128: hh * 1024 + (cc + 1) * 128],
                            opre[:, hh * 512:(hh + 1) * 512],
                            start=(hh == 0), stop=(hh == 1)), s_t if hh == 1 else None)
                    op_at = s_t.n
                    p.wait("vector", s_t, op_at)
                    if hp == 0:
                        p.op("vector", lambda e, ps=ps, cc=cc, win=win: e.tensor_scalar_add(
                            y_fm[:, cc * T + win * W: cc * T + (win + 1) * W], ps[:],
                            bout_sb[:, cc:cc + 1]), s_v)
                    else:
                        p.op("vector", lambda e, ps=ps, cc=cc, win=win: e.tensor_add(
                            y_fm[:, cc * T + win * W: cc * T + (win + 1) * W],
                            y_fm[:, cc * T + win * W: cc * T + (win + 1) * W], ps[:]), s_v)
                wofree[0] = (s_v, s_v.n)
        full_barrier()
        attn_done = s_v.n
        if "y_attn" in debug_parts:
            p.wait("sync", s_v, attn_done)
            p.dma("sync", dbg["y_attn"], y_fm[:], s_st)
            p.wait("vector", s_st, s_st.n)
            p.wait("tensor", s_st, s_st.n)
            p.wait("scalar", s_st, s_st.n)
            p.wait("gpsimd", s_st, s_st.n)
            p.wait("sync", s_st, s_st.n)
            p.wait("vector", s_st, s_st.n)
            p.wait("tensor", s_st, s_st.n)
            p.wait("sync", s_st, s_st.n)

        xb_free = [(s_v, attn_done), (s_v, attn_done)]
        obs = [ob0, ob1]
        ob_store_at = [None, None]
        for ti in range(NT):
            b = ti % 2
            if xb_free[b] is not None:
                p.wait("sync", *xb_free[b])
            p.dma("sync", xb[b][:], x_h[ti * 128:(ti + 1) * 128, :], s_in)
            xld_at = s_in.n
            p.wait("tensor", s_v, s_v.n)
            for cc in range(NC):
                p.op("tensor", lambda e, cc=cc, ti=ti: e.transpose(
                    psT[:, cc * 128:(cc + 1) * 128],
                    y_fm[:, cc * T + ti * 128: cc * T + ti * 128 + 128], eye_sb[:]),
                    s_t if cc == NC - 1 else None)
            ytp_at = s_t.n
            p.wait("vector", s_t, ytp_at)
            p.wait("vector", s_in, xld_at)
            if ob_store_at[b] is not None:
                p.wait("vector", s_st, ob_store_at[b])
            p.op("vector", lambda e, ob=obs[b], b=b: e.tensor_add(ob[:], xb[b][:], psT[:]), s_v)
            xb_free[b] = (s_v, s_v.n)
            add_at = s_v.n
            p.wait("sync", s_v, add_at)
            p.dma("sync", out_h[ti * 128:(ti + 1) * 128, :], obs[b][:], s_st)
            ob_store_at[b] = s_st.n
        partial_stores = s_st.n
        full_barrier()

        hid0_f = xn_fm[:, 0:8192]
        hid1_f = xn_fm[:, 8192:12288]
        xg0_fm = xn_fm[:, 12288:16384]
        xg1_fm = scr[:, 0:2048]
        eo0_fm = scr[:, 2048:6144]
        eo1_fm = scr[:, 6144:8192]
        xg_tm = [y_fm[:, 0:1024], y_fm[:, 1024:2048]]
        eo_tm = [y_fm[:, 2048:3072], y_fm[:, 3072:4096],
                 y_fm[:, 12288:13312], y_fm[:, 13312:14336],
                 y_fm[:, 14336:15360], y_fm[:, 15360:16384]]
        w12f = [y_fm[:, 4096:6144], y_fm[:, 6144:8192]]
        w3f = [y_fm[:, 8192:10240], y_fm[:, 10240:12288]]
        p.wait("sync", s_g, slot_scatters)
        p.dma("sync", slots_sb[:], slots_d.rearrange("(pp g) q -> pp (g q)", g=48), s_in)
        slots_rb = s_in.n
        p.wait("gpsimd", s_in, slots_rb)
        p.wait("gpsimd", s_st, partial_stores)
        w12i = 0
        w3i = 0
        w12free = [None, None]
        w3free = [None, None]
        xg_free = [None, None]
        expert_mark = None
        pending = []

        def flush_scatters():
            nonlocal_mark = expert_mark
            if nonlocal_mark is not None:
                p.wait("gpsimd", s_g, nonlocal_mark)
            for sti_, c_, sc_ in pending:
                p.wait("gpsimd", s_v, sc_)
                s_g.n += 16
                p.emit("gpsimd", lambda e, sti_=sti_, c_=c_: e.indirect_dma_start(
                    out_h, IOA(slots_sb[:, c_ * 4:c_ * 4 + 1], 0),
                    eo_tm[sti_][:], None, compute_op=ALU.add).then_inc(s_g.h, 16))
                if serial_dma:
                    p.wait("gpsimd", s_g, s_g.n)
            pending.clear()
            return s_g.n

        for e_ix in range(n_experts):
            cols = [e_ix * 6 + sti for sti in range(6)]

            def emit_gather(sti):
                b2 = sti % 2
                if xg_free[b2] is not None:
                    p.wait("gpsimd", s_t, xg_free[b2])
                s_g.n += 16
                p.emit("gpsimd", lambda e, c=cols[sti], b2=b2: e.indirect_dma_start(
                    xg_tm[b2][:], None, xn_d,
                    IOA(slots_sb[:, c * 4:c * 4 + 1], 0)).then_inc(s_g.h, 16))
                if serial_dma:
                    p.wait("gpsimd", s_g, s_g.n)
            for sti in range(2):
                emit_gather(sti)
            if pending:
                expert_mark = flush_scatters()
            for sti in range(6):
                b2 = sti % 2
                p.wait("tensor", s_g, s_g.n)
                p.wait("tensor", s_v, s_v.n)
                for cc in range(NC):
                    p.op("tensor", lambda e, cc=cc, b2=b2: e.transpose(
                        psT[:, cc * 128:(cc + 1) * 128],
                        xg_tm[b2][:, cc * 128:(cc + 1) * 128], eye_sb[:]),
                        s_t if cc == NC - 1 else None)
                xg_free[b2] = s_t.n
                if sti + 2 < 6:
                    emit_gather(sti + 2)
                p.wait("vector", s_t, s_t.n)
                if sti < 4:
                    p.op("vector", lambda e, sti=sti: e.tensor_copy(
                        xg0_fm.rearrange("p (dc t) -> p dc t", dc=NC)[
                            :, :, sti * 128:(sti + 1) * 128],
                        psT[:].rearrange("p (cc t) -> p cc t", cc=NC)), s_v)
                else:
                    p.op("vector", lambda e, sti=sti: e.tensor_copy(
                        xg1_fm.rearrange("p (dc t) -> p dc t", dc=NC)[
                            :, :, (sti - 4) * 128:(sti - 3) * 128],
                        psT[:].rearrange("p (cc t) -> p cc t", cc=NC)), s_v)
            xg_at = s_v.n
            bank_free = {}
            for hc in range(NHC):
                b = w12i % 2
                w12i += 1
                if w12free[b] is not None:
                    p.wait("sync", *w12free[b])
                    p.wait("gpsimd", *w12free[b])
                p.dma("sync", w12f[b][:, 0:C], w12t_h[e_ix, hc][:, 0:C], wsem[b])
                p.dma("gpsimd", w12f[b][:, C:2 * C], w12t_h[e_ix, hc][:, C:2 * C],
                      asem[b])
                w12_at = wsem[b].n
                w12a_at = asem[b].n
                ps2 = psA if hc % 2 == 0 else psB
                ps1 = psB if hc % 2 == 0 else psA
                ps2g = psC if hc % 2 == 0 else psD
                ps1g = psD if hc % 2 == 0 else psC
                p.wait("tensor", wsem[b], w12_at)
                p.wait("tensor", asem[b], w12a_at)
                if hc == 0:
                    p.wait("tensor", s_v, s_v.n)
                else:
                    if bank_free.get(id(ps2)) is not None:
                        p.wait("tensor", *bank_free[id(ps2)])
                    if bank_free.get(id(ps2g)) is not None:
                        p.wait("tensor", *bank_free[id(ps2g)])
                for dc in range(NC):
                    p.op("tensor", lambda e, dc=dc, b=b, ps2=ps2: e.matmul(
                        ps2[:, 0:512], w12f[b][:, C + dc * 128: C + (dc + 1) * 128],
                        xg0_fm[:, dc * 512:(dc + 1) * 512],
                        start=(dc == 0), stop=(dc == NC - 1)), s_t if dc == NC - 1 else None)
                for dc in range(NC):
                    p.op("tensor", lambda e, dc=dc, b=b, ps2g=ps2g: e.matmul(
                        ps2g[:, 0:256], w12f[b][:, C + dc * 128: C + (dc + 1) * 128],
                        xg1_fm[:, dc * 256:(dc + 1) * 256],
                        start=(dc == 0), stop=(dc == NC - 1)), s_t if dc == NC - 1 else None)
                h2_at = s_t.n
                if hc > 0:
                    if bank_free.get(id(ps1)) is not None:
                        p.wait("tensor", *bank_free[id(ps1)])
                    if bank_free.get(id(ps1g)) is not None:
                        p.wait("tensor", *bank_free[id(ps1g)])
                for dc in range(NC):
                    p.op("tensor", lambda e, dc=dc, b=b, ps1=ps1: e.matmul(
                        ps1[:, 0:512], w12f[b][:, dc * 128:(dc + 1) * 128],
                        xg0_fm[:, dc * 512:(dc + 1) * 512],
                        start=(dc == 0), stop=(dc == NC - 1)), s_t if dc == NC - 1 else None)
                h1g0_at = s_t.n
                for dc in range(NC):
                    p.op("tensor", lambda e, dc=dc, b=b, ps1g=ps1g: e.matmul(
                        ps1g[:, 0:256], w12f[b][:, dc * 128:(dc + 1) * 128],
                        xg1_fm[:, dc * 256:(dc + 1) * 256],
                        start=(dc == 0), stop=(dc == NC - 1)), s_t if dc == NC - 1 else None)
                h1_at = s_t.n
                p.wait("vector", s_t, h2_at)
                p.op("vector", lambda e, ps2=ps2, hc=hc: e.tensor_copy(
                    hid0_f[:, hc * 512:(hc + 1) * 512], ps2[:, 0:512]), s_v)
                p.op("vector", lambda e, ps2g=ps2g, hc=hc: e.tensor_copy(
                    hid1_f[:, hc * 256:(hc + 1) * 256], ps2g[:, 0:256]), s_v)
                bank_free[id(ps2)] = (s_v, s_v.n)
                bank_free[id(ps2g)] = (s_v, s_v.n)
                to = (hc % 2) * 512
                tg = (hc % 2) * 256
                if bank_free.get(("psTsilu", hc % 2)) is not None:
                    p.wait("scalar", *bank_free[("psTsilu", hc % 2)])
                p.wait("scalar", s_t, h1g0_at)
                p.op("scalar", lambda e, ps1=ps1, to=to: e.activation(
                    psT[:, to:to + 512], ps1[:, 0:512], AF.Silu), s_s)
                p.wait("scalar", s_t, h1_at)
                p.op("scalar", lambda e, ps1g=ps1g, tg=tg: e.activation(
                    psG[:, tg:tg + 256], ps1g[:, 0:256], AF.Silu), s_s)
                bank_free[id(ps1)] = (s_s, s_s.n)
                bank_free[id(ps1g)] = (s_s, s_s.n)
                p.wait("vector", s_s, s_s.n)
                p.op("vector", lambda e, hc=hc, to=to: e.tensor_mul(
                    hid0_f[:, hc * 512:(hc + 1) * 512], hid0_f[:, hc * 512:(hc + 1) * 512],
                    psT[:, to:to + 512]), s_v)
                p.op("vector", lambda e, hc=hc, tg=tg: e.tensor_mul(
                    hid1_f[:, hc * 256:(hc + 1) * 256], hid1_f[:, hc * 256:(hc + 1) * 256],
                    psG[:, tg:tg + 256]), s_v)
                bank_free[("psTsilu", hc % 2)] = (s_v, s_v.n)
                w12free[b] = (s_v, s_v.n)
            for dd in range(NC):
                b = w3i % 2
                w3i += 1
                if w3free[b] is not None:
                    p.wait("sync", *w3free[b])
                    p.wait("gpsimd", *w3free[b])
                p.dma("sync", w3f[b][:, 0:1024], w3t_h[e_ix, dd][:, 0:1024], wsem[b])
                p.dma("gpsimd", w3f[b][:, 1024:2048], w3t_h[e_ix, dd][:, 1024:2048],
                      asem[b])
                w3_at = wsem[b].n
                w3a_at = asem[b].n
                ps = psC if dd % 2 == 0 else psD
                psg = psE if dd % 2 == 0 else psG
                p.wait("tensor", wsem[b], w3_at)
                p.wait("tensor", asem[b], w3a_at)
                p.wait("tensor", s_v, s_v.n)
                for hc in range(NHC):
                    p.op("tensor", lambda e, hc=hc, b=b, ps=ps: e.matmul(
                        ps[:, 0:512], w3f[b][:, hc * 128:(hc + 1) * 128],
                        hid0_f[:, hc * 512:(hc + 1) * 512],
                        start=(hc == 0), stop=(hc == NHC - 1)), s_t if hc == NHC - 1 else None)
                for hc in range(NHC):
                    p.op("tensor", lambda e, hc=hc, b=b, psg=psg: e.matmul(
                        psg[:, 0:256], w3f[b][:, hc * 128:(hc + 1) * 128],
                        hid1_f[:, hc * 256:(hc + 1) * 256],
                        start=(hc == 0), stop=(hc == NHC - 1)), s_t if hc == NHC - 1 else None)
                w3mm_at = s_t.n
                p.wait("vector", s_t, w3mm_at)
                p.op("vector", lambda e, ps=ps, dd=dd: e.tensor_copy(
                    eo0_fm[:, dd * 512:(dd + 1) * 512], ps[:, 0:512]), s_v)
                p.op("vector", lambda e, psg=psg, dd=dd: e.tensor_copy(
                    eo1_fm[:, dd * 256:(dd + 1) * 256], psg[:, 0:256]), s_v)
                w3free[b] = (s_v, s_v.n)
            for sti in range(6):
                b2 = sti % 2
                p.wait("tensor", s_v, s_v.n)
                for dd in range(NC):
                    if sti < 4:
                        p.op("tensor", lambda e, dd=dd, sti=sti: e.transpose(
                            psT[:, dd * 128:(dd + 1) * 128],
                            eo0_fm[:, dd * 512 + sti * 128: dd * 512 + sti * 128 + 128],
                            eye_sb[:]), s_t if dd == NC - 1 else None)
                    else:
                        p.op("tensor", lambda e, dd=dd, sti=sti: e.transpose(
                            psT[:, dd * 128:(dd + 1) * 128],
                            eo1_fm[:, dd * 256 + (sti - 4) * 128:
                                   dd * 256 + (sti - 4) * 128 + 128],
                            eye_sb[:]), s_t if dd == NC - 1 else None)
                eot_at = s_t.n
                p.wait("vector", s_t, eot_at)
                if e_ix > 0:
                    p.wait("vector", s_g, s_g.n)
                p.op("vector", lambda e, sti=sti, c=cols[sti]: e.tensor_scalar_mul(
                    eo_tm[sti][:].rearrange("p (dd t) -> p dd t", dd=NC),
                    psT[:].rearrange("p (dd t) -> p dd t", dd=NC),
                    slots_sb[:, c * 4 + 1:c * 4 + 2].bitcast(F32)), s_v)
                sc_at = s_v.n
                if do_scatter:
                    pending.append((sti, cols[sti], sc_at))
        if pending:
            expert_mark = flush_scatters()
        p.wait("sync", s_st, s_st.n)
        p.wait("sync", s_g, s_g.n)

        @block.sync
        def _(eng):
            for f in p.q["sync"]:
                f(eng)

        @block.tensor
        def _(eng):
            for f in p.q["tensor"]:
                f(eng)

        @block.vector
        def _(eng):
            for f in p.q["vector"]:
                f(eng)

        @block.scalar
        def _(eng):
            for f in p.q["scalar"]:
                f(eng)

        @block.gpsimd
        def _(eng):
            for f in p.q["gpsimd"]:
                f(eng)

    return nc


_RAMP = np.tile(np.arange(8, 0, -1, dtype=np.float32), (128, 1))
_EYE = np.eye(128, dtype=np.float32)
_SEL = np.zeros((8, 8 * 128), dtype=np.float32)
for _e in range(8):
    _SEL[_e, _e * 128:(_e + 1) * 128] = 1.0
_TVALS = (np.arange(16)[None, :] * 128 + np.arange(128)[:, None]).astype(np.int32)
_SUT = np.triu(np.ones((128, 128), dtype=np.float32), k=1)
_ONESM = np.ones((128, 128), dtype=np.float32)


def kernel(L0_attn_x, L0_attn_w_qkv, L0_attn_b_qkv, L0_attn_w_out, L0_attn_b_out,
           L0_ffn_w12, L0_ffn_w3, L0_ffn_gate_w):
    x = np.ascontiguousarray(np.asarray(L0_attn_x, dtype=np.float32))
    wqkv = np.ascontiguousarray(np.asarray(L0_attn_w_qkv, dtype=np.float32))
    bqkv = np.ascontiguousarray(np.asarray(L0_attn_b_qkv, dtype=np.float32))
    wout = np.ascontiguousarray(np.asarray(L0_attn_w_out, dtype=np.float32))
    bout = np.ascontiguousarray(np.asarray(L0_attn_b_out, dtype=np.float32))
    w12t = np.asarray(L0_ffn_w12, dtype=np.float32).transpose(0, 2, 1)
    w3t = np.asarray(L0_ffn_w3, dtype=np.float32).transpose(0, 2, 1)
    w12t = np.ascontiguousarray(
        w12t.reshape(E, NC, 128, 2, NHC, 128).transpose(0, 4, 2, 3, 1, 5).reshape(
            E, NHC, 128, 2 * C))
    w3t = np.ascontiguousarray(
        w3t.reshape(E, NHC, 128, NC, 128).transpose(0, 3, 2, 1, 4).reshape(
            E, NC, 128, NHC * 128))
    gwt = np.ascontiguousarray(np.asarray(L0_ffn_gate_w, dtype=np.float32).T)
    bqkv_r = np.ascontiguousarray(bqkv.reshape(24, 128).T)
    bvrow = np.zeros((8, 1024), dtype=np.float32)
    bvrow[0] = bqkv[2048:3072]
    bout_r = np.ascontiguousarray(bout.reshape(8, 128).T)

    nc = build_program(debug=False)
    shared = {"wqkv": wqkv, "bqkv": bqkv_r, "bvrow": bvrow, "wout": wout, "bout": bout_r,
              "w12t": w12t, "w3t": w3t, "gwt": gwt,
              "eye": _EYE, "ramp": _RAMP, "sel": _SEL,
              "tvals": _TVALS, "sut": _SUT, "onesm": _ONESM}
    in_maps = [dict(shared, x=np.ascontiguousarray(x[i])) for i in range(8)]
    res = run_bass_kernel_spmd(nc, in_maps, core_ids=list(range(8)))
    out = np.stack([res.results[i]["out"] for i in range(8)], axis=0)
    return out.astype(np.float32)


# version 12
# speedup vs baseline: 1.0782x; 1.0782x over previous
import numpy as np
import concourse.bass as bass
import concourse.mybir as mybir
from concourse.bass_utils import run_bass_kernel_spmd

F32 = mybir.dt.float32
ALU = mybir.AluOpType
AF = mybir.ActivationFunctionType
AX = mybir.AxisListType.X

T = 2048
C = 1024
NT = T // 128
NC = C // 128
W = 512
NWIN = 4
STRIDE = 256
E = 8
HID = 2048
NHC = HID // 128


class Sem:
    def __init__(self, h):
        self.h = h
        self.n = 0


class Plan:
    def __init__(self):
        self.q = {k: [] for k in ("sync", "tensor", "vector", "scalar", "gpsimd")}

    def emit(self, eng, fn):
        self.q[eng].append(fn)

    def wait(self, eng, sem, val=None):
        v = sem.n if val is None else val
        assert v <= sem.n, f"wait for future value {v} > {sem.n}"
        if v > 0:
            self.emit(eng, lambda e, s=sem.h, v=v: e.wait_ge(s, v))

    def dma(self, eng, out, in_, sem):
        sem.n += 16
        self.emit(eng, lambda e, o=out, i=in_, s=sem.h: e.dma_start(out=o, in_=i).then_inc(s, 16))

    def op(self, eng, fn, sem=None):
        if sem is not None:
            sem.n += 1
            self.emit(eng, lambda e, f=fn, s=sem.h: f(e).then_inc(s, 1))
        else:
            self.emit(eng, lambda e, f=fn: f(e))


def build_program(debug=False, debug_parts=None, n_experts=E, do_scatter=True, serial_dma=False):
    nc = bass.Bass("TRN2", target_bir_lowering=False, detect_race_conditions=False)

    x_h = nc.dram_tensor("x", [T, C], F32, kind="ExternalInput").ap()
    wqkv_h = nc.dram_tensor("wqkv", [C, 3 * C], F32, kind="ExternalInput").ap()
    bqkv_h = nc.dram_tensor("bqkv", [128, 24], F32, kind="ExternalInput").ap()
    bvrow_h = nc.dram_tensor("bvrow", [8, C], F32, kind="ExternalInput").ap()
    wout_h = nc.dram_tensor("wout", [C, C], F32, kind="ExternalInput").ap()
    bout_h = nc.dram_tensor("bout", [128, NC], F32, kind="ExternalInput").ap()
    w12t_h = nc.dram_tensor("w12t", [E, NHC, 128, 2 * C], F32, kind="ExternalInput").ap()
    w3t_h = nc.dram_tensor("w3t", [E, NC, 128, NHC * 128], F32, kind="ExternalInput").ap()
    gwt_h = nc.dram_tensor("gwt", [C, E], F32, kind="ExternalInput").ap()
    eye_h = nc.dram_tensor("eye", [128, 128], F32, kind="ExternalInput").ap()
    ramp_h = nc.dram_tensor("ramp", [128, E], F32, kind="ExternalInput").ap()
    sel_h = nc.dram_tensor("sel", [8, 8 * 128], F32, kind="ExternalInput").ap()
    I32 = mybir.dt.int32
    tvals_h = nc.dram_tensor("tvals", [128, 16], I32, kind="ExternalInput").ap()
    sut_h = nc.dram_tensor("sut", [128, 128], F32, kind="ExternalInput").ap()
    onesm_h = nc.dram_tensor("onesm", [128, 128], F32, kind="ExternalInput").ap()
    xn_d = nc.dram_tensor("xn_scratch", [T, C], F32, kind="Internal").ap()
    slots_d = nc.dram_tensor("slots_scratch", [6144, 4], I32, kind="Internal").ap()
    out_h = nc.dram_tensor("out", [T, C], F32, kind="ExternalOutput").ap()
    dbg = {}
    if debug_parts is None:
        debug_parts = {"xn_fm", "p_fm", "y_attn", "qkv", "prob", "pt", "opre", "qkv3", "stat"} if debug else set()
    debug = bool(debug_parts)
    if debug:
        dbg["xn_fm"] = nc.dram_tensor("dbg_xn_fm", [128, NC * T], F32, kind="ExternalOutput").ap()
        dbg["p_fm"] = nc.dram_tensor("dbg_p_fm", [8, T], F32, kind="ExternalOutput").ap()
        dbg["y_attn"] = nc.dram_tensor("dbg_y_attn", [128, NC * T], F32, kind="ExternalOutput").ap()
        dbg["qkv"] = nc.dram_tensor("dbg_qkv", [128, 3 * W], F32, kind="ExternalOutput").ap()
        dbg["prob"] = nc.dram_tensor("dbg_prob", [128, W], F32, kind="ExternalOutput").ap()
        dbg["pt"] = nc.dram_tensor("dbg_pt", [128, 4 * W], F32, kind="ExternalOutput").ap()
        dbg["opre"] = nc.dram_tensor("dbg_opre", [64, 1024], F32, kind="ExternalOutput").ap()
        dbg["qkv3"] = nc.dram_tensor("dbg_qkv3", [128, 3 * W], F32, kind="ExternalOutput").ap()
        dbg["stat"] = nc.dram_tensor("dbg_stat", [128, 224], F32, kind="ExternalOutput").ap()
        dbg["xg"] = nc.dram_tensor("dbg_xg", [128, 4096], F32, kind="ExternalOutput").ap()
        dbg["xgtm"] = nc.dram_tensor("dbg_xgtm", [128, 2048], F32, kind="ExternalOutput").ap()
        dbg["eo"] = nc.dram_tensor("dbg_eo", [128, 4096], F32, kind="ExternalOutput").ap()
        dbg["hid"] = nc.dram_tensor("dbg_hid", [128, 8192], F32, kind="ExternalOutput").ap()
        dbg["w12l"] = nc.dram_tensor("dbg_w12l", [128, 4096], F32, kind="ExternalOutput").ap()

    p = Plan()

    from contextlib import ExitStack
    with ExitStack() as _stack:
        _ec = _stack.enter_context
        xn_fm = _ec(nc.sbuf_tensor("xn_fm", [128, NC * T], F32))
        y_fm = _ec(nc.sbuf_tensor("y_fm", [128, NC * T], F32))
        scr = _ec(nc.sbuf_tensor("scr", [128, 16896], F32))
        bqkv_sb = _ec(nc.sbuf_tensor("bqkv_sb", [128, 24], F32))
        gwt_sb = _ec(nc.sbuf_tensor("gwt_sb", [128, NC * E], F32))
        p_fm = _ec(nc.sbuf_tensor("p_fm", [8, T], F32))
        stat = _ec(nc.sbuf_tensor("stat", [128, 224], F32))
        eye_sb = _ec(nc.sbuf_tensor("eye_sb", [128, 128], F32))
        ramp_sb = _ec(nc.sbuf_tensor("ramp_sb", [128, E], F32))
        sel_sb = _ec(nc.sbuf_tensor("sel_sb", [8, 8 * 128], F32))
        tvals_sb = _ec(nc.sbuf_tensor("tvals_sb", [128, 16], I32))
        bout_sb = _ec(nc.sbuf_tensor("bout_sb", [128, NC], F32))
        xt0 = scr[:, 12800:13824]
        xt1 = scr[:, 13824:14848]
        ob0 = scr[:, 14848:15872]
        ob1 = scr[:, 15872:16896]
        sqjunk = scr[:, 14848:15872]
        xnt0 = scr[:, 14848:15872]
        xnt1 = scr[:, 15872:16896]
        sut_sb = scr[:, 3072:3200]
        onesm_sb = scr[:, 3200:3328]
        quads_sb = scr[:, 3328:3456].bitcast(mybir.dt.int32)
        sidxf_sb = scr[:, 3456:3536]
        sidx_sb = scr[:, 3536:3584].bitcast(mybir.dt.int32)
        scg_sb = scr[:, 0:NT * E]
        pw_sb = scr[:, 256:256 + NT * E]
        z1b = scr[:, 1024:1152]
        one1b = scr[:, 1152:1280]
        sc2b = scr[:, 1280:1408]
        z2b = scr[:, 1408:1536]
        one2b = scr[:, 1536:1664]
        wq_sl0 = scr[:, 0:1024]
        wq_sl1 = scr[:, 1024:2048]
        wk_sl0 = scr[:, 2048:3072]
        wk_sl1 = scr[:, 3072:4096]
        wv_sl0 = scr[:, 4096:5120]
        wv_sl1 = scr[:, 5120:6144]
        wo_sl0 = scr[0:64, 6144:8192]
        q_sb = scr[:, 8192:8704]
        k_sb = scr[:, 8704:9216]
        v_sb = scr[:, 9216:9728]
        opre = scr[0:64, 9728:10752]
        prob = scr[:, 10752:11264]
        pt_sb = scr[:, 11264:13312]
        bv_sb = scr[0:8, 15872:16896]
        prob_b = [scr[:, 10752:11264], scr[:, 13312:13824]]
        pt_b = [scr[:, 11264:13312], scr[:, 13824:15872]]
        hid_sb = scr[:, 0:NHC * W]
        w12_sl0 = scr[:, 8192:10240]
        w12_sl1 = scr[:, 10240:12288]
        w3_sl0 = scr[:, 12288:14336]
        w3_sl1 = scr[:, 14336:16384]
        pb_sb = scr[:, 16384:16896]
        psA = _ec(nc.psum_tensor("psA", [128, 512], F32))
        psB = _ec(nc.psum_tensor("psB", [128, 512], F32))
        psC = _ec(nc.psum_tensor("psC", [128, 512], F32))
        psD = _ec(nc.psum_tensor("psD", [128, 512], F32))
        psE = _ec(nc.psum_tensor("psE", [128, 512], F32))
        psT = _ec(nc.psum_tensor("psT", [128, 1024], F32))
        psG = _ec(nc.psum_tensor("psG", [128, 512], F32))
        _s_in = _ec(nc.semaphore("s_in"))
        _s_w0 = _ec(nc.semaphore("s_w0"))
        _s_w1 = _ec(nc.semaphore("s_w1"))
        _s_t = _ec(nc.semaphore("s_t"))
        _s_v = _ec(nc.semaphore("s_v"))
        _s_s = _ec(nc.semaphore("s_s"))
        _s_st = _ec(nc.semaphore("s_st"))
        _s_g = _ec(nc.semaphore("s_g"))
        _s_a0 = _ec(nc.semaphore("s_a0"))
        _s_a1 = _ec(nc.semaphore("s_a1"))
        block = _ec(nc.Block())
        s_in = Sem(_s_in)
        s_w0 = Sem(_s_w0)
        s_w1 = Sem(_s_w1)
        s_t = Sem(_s_t)
        s_v = Sem(_s_v)
        s_s = Sem(_s_s)
        s_st = Sem(_s_st)
        s_g = Sem(_s_g)
        s_a0 = Sem(_s_a0)
        s_a1 = Sem(_s_a1)
        slots_sb = scr[:, 8192:8384].bitcast(mybir.dt.int32)
        wsem = [s_w0, s_w1]
        asem = [s_a0, s_a1]

        def vgap():
            p.op("vector", lambda e: e.tensor_copy(stat[:, 220:221], stat[:, 221:222]))

        def sgap():
            p.op("scalar", lambda e: e.copy(stat[:, 222:223], stat[:, 221:222]))

        def vgap2():
            p.op("vector", lambda e: e.tensor_copy(scr[:, 15104:15360], scr[:, 14848:15104]))
            p.op("vector", lambda e: e.tensor_copy(scr[:, 15104:15360], scr[:, 14848:15104]))

        S_SUM, S_SS, S_MEAN, S_VAR, S_VPE, S_SD, S_R0, S_U, S_RSTD, S_NMR = (
            0, 20, 40, 60, 80, 100, 120, 140, 160, 180)

        def full_barrier():
            p.op("vector", lambda e: e.tensor_copy(stat[:, 220:221], stat[:, 221:222]), s_v)
            p.op("scalar", lambda e: e.copy(stat[:, 222:223], stat[:, 221:222]), s_s)
            for eng in ("vector", "scalar", "tensor", "sync", "gpsimd"):
                for sem in (s_v, s_s, s_t, s_in, s_w0, s_w1, s_a0, s_a1, s_st):
                    p.wait(eng, sem, sem.n)

        p.op("vector", lambda e: e.memset(stat[:], 0.0))
        p.dma("sync", eye_sb[:], eye_h, s_in)
        p.dma("sync", ramp_sb[:], ramp_h, s_in)
        p.dma("sync", sel_sb[:], sel_h, s_in)
        p.dma("sync", tvals_sb[:], tvals_h, s_in)
        p.dma("sync", sut_sb[:], sut_h, s_in)
        p.dma("sync", onesm_sb[:], onesm_h, s_in)
        p.dma("sync", bqkv_sb[:], bqkv_h, s_in)
        p.dma("sync", bout_sb[:], bout_h, s_in)
        p.dma("sync", gwt_sb[:].rearrange("p (dc e) -> p dc e", dc=NC),
              gwt_h.rearrange("(dc p) e -> p dc e", p=128), s_in)

        xb = [xt0, xt1]
        xb_free = [None, None]
        for ti in range(NT):
            b = ti % 2
            if xb_free[b] is not None:
                p.wait("sync", *xb_free[b])
            p.dma("sync", xb[b][:], x_h[ti * 128:(ti + 1) * 128, :], s_in)
            ld_at = s_in.n
            p.wait("vector", s_in, ld_at)
            p.op("vector", lambda e, xb=xb[b], ti=ti: e.reduce_sum(
                stat[:, S_SUM + ti:S_SUM + ti + 1], xb[:], axis=AX))
            p.op("vector", lambda e, xb=xb[b]: e.tensor_mul(sqjunk[:], xb[:], xb[:]))
            p.op("vector", lambda e, ti=ti: e.reduce_sum(
                stat[:, S_SS + ti:S_SS + ti + 1], sqjunk[:], axis=AX), s_v)
            xb_free[b] = (s_v, s_v.n)
        vgap2()
        vgap2()
        ST = lambda g: stat[:, g:g + 20]
        p.op("vector", lambda e: e.tensor_scalar_mul(ST(S_MEAN), ST(S_SUM), 1.0 / C))
        vgap2()
        p.op("vector", lambda e: e.tensor_mul(ST(S_U), ST(S_MEAN), ST(S_MEAN)))
        vgap2()
        p.op("vector", lambda e: e.scalar_tensor_tensor(ST(S_VAR), ST(S_SS), 1.0 / C, ST(S_U),
                                                        ALU.mult, ALU.subtract))
        vgap2()
        p.op("vector", lambda e: e.tensor_scalar_add(ST(S_VPE), ST(S_VAR), 1e-5), s_v)
        p.wait("scalar", s_v, s_v.n)
        sgap()
        sgap()
        p.op("scalar", lambda e: e.activation(ST(S_SD), ST(S_VPE), AF.Sqrt), s_s)
        p.wait("vector", s_s, s_s.n)
        vgap2()
        p.op("vector", lambda e: e.reciprocal(ST(S_R0), ST(S_SD)))
        vgap2()
        p.op("vector", lambda e: e.tensor_mul(ST(S_U), ST(S_R0), ST(S_R0)))
        vgap2()
        p.op("vector", lambda e: e.tensor_mul(ST(S_U), ST(S_U), ST(S_VPE)))
        vgap2()
        p.op("vector", lambda e: e.tensor_scalar(ST(S_U), ST(S_U), -0.5, 1.5, ALU.mult, ALU.add))
        vgap2()
        p.op("vector", lambda e: e.tensor_mul(ST(S_RSTD), ST(S_R0), ST(S_U)))
        vgap2()
        p.op("vector", lambda e: e.tensor_mul(ST(S_NMR), ST(S_MEAN), ST(S_RSTD)))
        vgap2()
        p.op("vector", lambda e: e.tensor_scalar_mul(ST(S_NMR), ST(S_NMR), -1.0))
        vgap2()
        p.op("vector", lambda e: e.tensor_copy(stat[:, 220:221], stat[:, 221:222]), s_v)
        p.wait("sync", s_v, s_v.n)
        p.dma("sync", scr[:, 4096:4320], stat[:], s_in)
        stats_flush = s_in.n
        stats_at = s_v.n
        if "stat" in debug_parts:
            p.wait("sync", s_v, stats_at)
            p.dma("sync", dbg["stat"], stat[:], s_st)
            p.wait("vector", s_st, s_st.n)
            p.wait("tensor", s_st, s_st.n)
            p.wait("scalar", s_st, s_st.n)
            p.wait("gpsimd", s_st, s_st.n)
            p.wait("sync", s_st, s_st.n)
            p.wait("vector", s_st, s_st.n)
            p.wait("tensor", s_st, s_st.n)
            p.wait("sync", s_st, s_st.n)

        xnb = [xnt0, xnt1]
        xnb_free = [None, None]
        xb_free = [None, None]
        for ti in range(NT):
            b = ti % 2
            if xb_free[b] is not None:
                p.wait("sync", *xb_free[b])
            p.dma("sync", xb[b][:], x_h[ti * 128:(ti + 1) * 128, :], s_in)
            ld_at = s_in.n
            p.wait("vector", s_in, ld_at)
            if ti == 0:
                p.wait("vector", s_v, stats_at)
                p.wait("vector", s_in, stats_flush)
            if xnb_free[b] is not None:
                p.wait("vector", *xnb_free[b])
                p.wait("vector", s_in, s_in.n)
            p.op("vector", lambda e, xb=xb[b], xn=xnb[b], ti=ti: e.tensor_scalar(
                xn[:], xb[:], stat[:, S_RSTD + ti:S_RSTD + ti + 1],
                stat[:, S_NMR + ti:S_NMR + ti + 1], ALU.mult, ALU.add), s_v)
            xb_free[b] = (s_v, s_v.n)
            xnt_at = s_v.n
            p.wait("sync", s_v, xnt_at)
            p.dma("sync", xn_d[ti * 128:(ti + 1) * 128, :], xnb[b][:], s_in)
            xn_st_at = s_in.n
            xnb_free2 = (s_in, xn_st_at)
            p.wait("tensor", s_v, xnt_at)
            for cc in range(NC):
                p.op("tensor", lambda e, cc=cc, xn=xnb[b]: e.transpose(
                    psT[:, cc * 128:(cc + 1) * 128], xn[:, cc * 128:(cc + 1) * 128], eye_sb[:]),
                    s_t if cc == NC - 1 else None)
            tp_at = s_t.n
            xnb_free[b] = (s_t, tp_at)
            xnb_hbm_free = xnb_free2
            p.wait("vector", s_t, tp_at)
            p.op("vector", lambda e, ti=ti: e.tensor_copy(
                xn_fm[:].rearrange("p (cc t) -> p cc t", cc=NC)[:, :, ti * 128:(ti + 1) * 128],
                psT[:].rearrange("p (cc t) -> p cc t", cc=NC)), s_v)
            evac_at = s_v.n
            p.wait("tensor", s_v, evac_at)
            for dc in range(NC):
                p.op("tensor", lambda e, dc=dc, ti=ti: e.matmul(
                    psG[:, ti * 8: ti * 8 + 8],
                    xn_fm[:, dc * T + ti * 128: dc * T + ti * 128 + 128],
                    gwt_sb[:, dc * E:(dc + 1) * E],
                    start=(ti == 0 and dc == 0), stop=(ti == NT - 1 and dc == NC - 1)),
                    s_t if (dc == NC - 1) else None)
        gate_done = s_t.n

        if "xn_fm" in debug_parts:
            p.wait("sync", s_v, s_v.n)
            p.dma("sync", dbg["xn_fm"], xn_fm[:], s_st)
            p.wait("vector", s_st, s_st.n)
            p.wait("tensor", s_st, s_st.n)
            p.wait("scalar", s_st, s_st.n)
            p.wait("gpsimd", s_st, s_st.n)
            p.wait("sync", s_st, s_st.n)
            p.wait("vector", s_st, s_st.n)
            p.wait("tensor", s_st, s_st.n)
            p.wait("sync", s_st, s_st.n)

        M1, R1, M2, R2, D12, P1, P2 = 0, 20, 40, 60, 80, 100, 120
        p.wait("vector", s_t, gate_done)
        p.op("vector", lambda e: e.tensor_copy(scg_sb[:], psG[:, 0:NT * E]), s_v)
        vgap()
        for ti in range(NT):
            p.op("vector", lambda e, ti=ti: e.reduce_max(
                stat[:, M1 + ti:M1 + ti + 1], scg_sb[:, ti * E:(ti + 1) * E], axis=AX))
        for ti in range(NT):
            p.op("vector", lambda e, ti=ti: e.tensor_scalar(
                z1b[:, ti * E:(ti + 1) * E], scg_sb[:, ti * E:(ti + 1) * E],
                stat[:, M1 + ti:M1 + ti + 1], None, ALU.is_equal))
        for ti in range(NT):
            p.op("vector", lambda e, ti=ti: e.tensor_mul(
                z1b[:, ti * E:(ti + 1) * E], z1b[:, ti * E:(ti + 1) * E], ramp_sb[:]))
        for ti in range(NT):
            p.op("vector", lambda e, ti=ti: e.reduce_max(
                stat[:, R1 + ti:R1 + ti + 1], z1b[:, ti * E:(ti + 1) * E], axis=AX))
        for ti in range(NT):
            p.op("vector", lambda e, ti=ti: e.tensor_scalar(
                one1b[:, ti * E:(ti + 1) * E], ramp_sb[:],
                stat[:, R1 + ti:R1 + ti + 1], None, ALU.is_equal))
        for ti in range(NT):
            p.op("vector", lambda e, ti=ti: e.scalar_tensor_tensor(
                sc2b[:, ti * E:(ti + 1) * E], one1b[:, ti * E:(ti + 1) * E], -1e30,
                scg_sb[:, ti * E:(ti + 1) * E], ALU.mult, ALU.add))
        for ti in range(NT):
            p.op("vector", lambda e, ti=ti: e.reduce_max(
                stat[:, M2 + ti:M2 + ti + 1], sc2b[:, ti * E:(ti + 1) * E], axis=AX))
        for ti in range(NT):
            p.op("vector", lambda e, ti=ti: e.tensor_scalar(
                z2b[:, ti * E:(ti + 1) * E], sc2b[:, ti * E:(ti + 1) * E],
                stat[:, M2 + ti:M2 + ti + 1], None, ALU.is_equal))
        for ti in range(NT):
            p.op("vector", lambda e, ti=ti: e.tensor_mul(
                z2b[:, ti * E:(ti + 1) * E], z2b[:, ti * E:(ti + 1) * E], ramp_sb[:]))
        for ti in range(NT):
            p.op("vector", lambda e, ti=ti: e.reduce_max(
                stat[:, R2 + ti:R2 + ti + 1], z2b[:, ti * E:(ti + 1) * E], axis=AX))
        for ti in range(NT):
            p.op("vector", lambda e, ti=ti: e.tensor_scalar(
                one2b[:, ti * E:(ti + 1) * E], ramp_sb[:],
                stat[:, R2 + ti:R2 + ti + 1], None, ALU.is_equal))
        p.op("vector", lambda e: e.tensor_sub(stat[:, D12:D12 + 20], stat[:, M1:M1 + 20],
                                              stat[:, M2:M2 + 20]), s_v)
        p.wait("scalar", s_v, s_v.n)
        sgap()
        sgap()
        p.op("scalar", lambda e: e.activation(stat[:, P1:P1 + 20], stat[:, D12:D12 + 20],
                                              AF.Sigmoid), s_s)
        p.wait("vector", s_s, s_s.n)
        vgap()
        p.op("vector", lambda e: e.tensor_scalar(stat[:, P2:P2 + 20], stat[:, P1:P1 + 20],
                                                 -1.0, 1.0, ALU.mult, ALU.add))
        vgap()
        for ti in range(NT):
            p.op("vector", lambda e, ti=ti: e.tensor_scalar_mul(
                pw_sb[:, ti * E:(ti + 1) * E], one1b[:, ti * E:(ti + 1) * E],
                stat[:, P1 + ti:P1 + ti + 1]))
        for ti in range(NT):
            p.op("vector", lambda e, ti=ti: e.scalar_tensor_tensor(
                pw_sb[:, ti * E:(ti + 1) * E], one2b[:, ti * E:(ti + 1) * E],
                stat[:, P2 + ti:P2 + ti + 1], pw_sb[:, ti * E:(ti + 1) * E],
                ALU.mult, ALU.add), None)
        CAP = 768
        mb = scr[:, 1664:1792]
        rb = scr[:, 1792:1920]
        tm8 = scr[:, 1920:2048]
        RANK1, RANK2 = 140, 160
        E1g, E2g, G1g, G2g, RR1, RR2 = 0, 40, 20, 60, 180, 200
        for ti in range(NT):
            p.op("vector", lambda e, ti=ti: e.tensor_add(
                mb[:, ti * E:(ti + 1) * E], one1b[:, ti * E:(ti + 1) * E],
                one2b[:, ti * E:(ti + 1) * E]), None)
        p.op("vector", lambda e: e.tensor_copy(stat[:, 220:221], stat[:, 221:222]), s_v)
        pw_done = s_v.n
        p.wait("sync", s_v, pw_done)
        p.dma("sync", scr[:, 5632:5760], scr[:, 256:384], s_in)
        p.dma("sync", scr[:, 5760:6528], scr[:, 1024:1792], s_in)
        pw_flush = s_in.n
        p.wait("tensor", s_in, pw_flush)
        first = True
        for i in range(NT):
            for j in range(i + 1):
                lhs = sut_sb if j == i else onesm_sb
                p.op("tensor", lambda e, i=i, j=j, lhs=lhs, st=first: e.matmul(
                    psE[:, i * 8:(i + 1) * 8], lhs[:], mb[:, j * E:(j + 1) * E],
                    start=st, stop=(i == NT - 1 and j == i)),
                    s_t if (i == NT - 1 and j == i) else None)
                first = False
        rmm_at = s_t.n
        p.wait("vector", s_t, rmm_at)
        p.op("vector", lambda e: e.tensor_copy(rb[:], psE[:, 0:128]), s_v)
        vgap()
        for ti in range(NT):
            p.op("vector", lambda e, ti=ti: e.tensor_mul(
                tm8[:, ti * E:(ti + 1) * E], rb[:, ti * E:(ti + 1) * E],
                one1b[:, ti * E:(ti + 1) * E]))
        for ti in range(NT):
            p.op("vector", lambda e, ti=ti: e.reduce_sum(
                stat[:, RANK1 + ti:RANK1 + ti + 1], tm8[:, ti * E:(ti + 1) * E], axis=AX))
        for ti in range(NT):
            p.op("vector", lambda e, ti=ti: e.tensor_mul(
                tm8[:, ti * E:(ti + 1) * E], rb[:, ti * E:(ti + 1) * E],
                one2b[:, ti * E:(ti + 1) * E]))
        for ti in range(NT):
            p.op("vector", lambda e, ti=ti: e.reduce_sum(
                stat[:, RANK2 + ti:RANK2 + ti + 1], tm8[:, ti * E:(ti + 1) * E], axis=AX))
        vgap2()
        p.op("vector", lambda e: e.tensor_scalar(stat[:, E1g:E1g + 20], stat[:, R1:R1 + 20],
                                                 -1.0, 8.0, ALU.mult, ALU.add))
        p.op("vector", lambda e: e.tensor_scalar(stat[:, E2g:E2g + 20], stat[:, R2:R2 + 20],
                                                 -1.0, 8.0, ALU.mult, ALU.add))
        vgap2()
        scrA = sidxf_sb[:, 40:60]
        scrB = sidxf_sb[:, 60:80]
        for kk, (RK, Gg) in enumerate(((RANK1, G1g), (RANK2, G2g))):
            p.op("vector", lambda e, RK=RK, Gg=Gg: e.tensor_scalar(
                stat[:, Gg:Gg + 20], stat[:, RK:RK + 20], 128.0, None, ALU.is_ge))
        vgap2()
        for m in range(2, 6):
            for kk, (RK, Gg) in enumerate(((RANK1, G1g), (RANK2, G2g))):
                p.op("vector", lambda e, RK=RK, m=m: e.tensor_scalar(
                    scrA, stat[:, RK:RK + 20], float(m * 128), None, ALU.is_ge))
                vgap2()
                p.op("vector", lambda e, Gg=Gg: e.tensor_add(
                    stat[:, Gg:Gg + 20], stat[:, Gg:Gg + 20], scrA))
                vgap2()
        for kk, (RK, Gg, RRg) in enumerate(((RANK1, G1g, RR1), (RANK2, G2g, RR2))):
            p.op("vector", lambda e, RK=RK, Gg=Gg, RRg=RRg: e.scalar_tensor_tensor(
                stat[:, RRg:RRg + 20], stat[:, Gg:Gg + 20], -128.0, stat[:, RK:RK + 20],
                ALU.mult, ALU.add))
        vgap2()
        for kk, (Eg, Gg, RRg, o) in enumerate(((E1g, G1g, RR1, 0), (E2g, G2g, RR2, 16))):
            p.op("vector", lambda e, Eg=Eg, Gg=Gg: e.scalar_tensor_tensor(
                scrB[:, 0:16], stat[:, Eg:Eg + 16], 6.0, stat[:, Gg:Gg + 16],
                ALU.mult, ALU.add))
            vgap2()
            p.op("vector", lambda e, RRg=RRg, o=o: e.scalar_tensor_tensor(
                sidxf_sb[:, o:o + 16], stat[:, RRg:RRg + 16], 48.0,
                scrB[:, 0:16], ALU.mult, ALU.add))
            vgap2()
        p.op("vector", lambda e: e.memset(quads_sb[:], 0))
        qv = quads_sb[:].rearrange("p (g q) -> p g q", q=4)
        for k in range(2):
            Pg = P1 if k == 0 else P2
            p.op("vector", lambda e, k=k, qv=qv: e.tensor_copy(
                qv[:, k * 16:(k + 1) * 16, 0:1], tvals_sb[:].rearrange("p (t o) -> p t o", o=1)))
            p.op("vector", lambda e, k=k, qv=qv, Pg=Pg: e.tensor_copy(
                qv[:, k * 16:(k + 1) * 16, 1:2].bitcast(F32),
                stat[:, Pg:Pg + 16].rearrange("p (t o) -> p t o", o=1)))
        p.op("vector", lambda e: e.memset(slots_sb[:], 0), s_v)
        quads_at = s_v.n
        p.wait("sync", s_v, quads_at)
        p.dma("sync", slots_d.rearrange("(pp g) q -> pp (g q)", g=48), slots_sb[:], s_in)
        slots_zero = s_in.n
        p.dma("sync", scr[:, 7424:7456], sidxf_sb[:, 0:32], s_in)
        sidx_flush = s_in.n
        p.wait("gpsimd", s_in, sidx_flush)
        p.op("gpsimd", lambda e: e.tensor_copy(sidx_sb[:, 0:32], sidxf_sb[:, 0:32]), s_s)
        sidx_conv = s_s.n
        p.wait("gpsimd", s_in, slots_zero)
        from concourse.bass import IndirectOffsetOnAxis as IOA
        for k in range(2):
            for ti in range(NT):
                col = k * 16 + ti
                s_g.n += 16
                p.emit("gpsimd", lambda e, col=col, qv=qv: e.indirect_dma_start(
                    slots_d, IOA(sidx_sb[:, col:col + 1], 0),
                    quads_sb[:, col * 4:(col + 1) * 4], None).then_inc(s_g.h, 16))
        slot_scatters = s_g.n
        pw_done = s_v.n
        p.wait("tensor", s_v, pw_done)
        for rnd in range(2):
            for j in range(8):
                ti = rnd * 8 + j
                p.op("tensor", lambda e, ti=ti, j=j: e.transpose(
                    psT[0:8, j * 128:(j + 1) * 128], pw_sb[:, ti * E:(ti + 1) * E], eye_sb[:]),
                    s_t if j == 7 else None)
            p.wait("vector", s_t, s_t.n)
            p.op("vector", lambda e, rnd=rnd: e.tensor_copy(
                p_fm[:, rnd * 1024:(rnd + 1) * 1024], psT[0:8, :]), s_v)
            if rnd == 0:
                p.wait("tensor", s_v, s_v.n)
        full_barrier()
        routing_done = s_v.n
        if "p_fm" in debug_parts:
            p.wait("sync", s_v, routing_done)
            p.dma("sync", dbg["p_fm"], p_fm[:], s_st)
            p.wait("vector", s_st, s_st.n)
            p.wait("tensor", s_st, s_st.n)
            p.wait("scalar", s_st, s_st.n)
            p.wait("gpsimd", s_st, s_st.n)
            p.wait("sync", s_st, s_st.n)
            p.wait("vector", s_st, s_st.n)
            p.wait("tensor", s_st, s_st.n)
            p.wait("sync", s_st, s_st.n)

        def load_wslice(dst, col0, ncols, sem):
            p.dma("sync", dst[:, 0:NC * ncols].rearrange("p (dc j) -> p dc j", dc=NC),
                  wqkv_h.rearrange("(dc p) m -> p dc m", p=128)[:, :, col0:col0 + ncols], sem)

        wq_b = [wq_sl0, wq_sl1]
        wk_b = [wk_sl0, wk_sl1]
        wv_b = [wv_sl0, wv_sl1]
        wfree = [(s_v, routing_done), (s_v, routing_done)]
        wofree = [None]
        p.wait("sync", s_v, routing_done)
        p.wait("sync", s_g, slot_scatters)
        p.dma("sync", bv_sb[:], bvrow_h, s_in)
        bv_at = s_in.n

        for hp in range(8):
            b = hp % 2
            if wfree[b] is not None:
                p.wait("sync", *wfree[b])
            load_wslice(wq_b[b], hp * 128, 128, wsem[b])
            load_wslice(wk_b[b], C + hp * 128, 128, wsem[b])
            load_wslice(wv_b[b], 2 * C + hp * 128, 128, wsem[b])
            wld_at = wsem[b].n
            if wofree[0] is not None:
                p.wait("sync", *wofree[0])
            p.dma("sync", wo_sl0[:].rearrange("p (hh m) -> p hh m", hh=2),
                  wout_h.rearrange("(g p) m -> g p m", p=64)[2 * hp:2 * hp + 2].rearrange(
                      "g p m -> p g m"), wsem[0])
            wo_at = wsem[0].n
            for win in range(NWIN):
                t0 = win * STRIDE

                p.wait("tensor", wsem[b], wld_at)
                if win == 0 and hp == 0:
                    p.wait("tensor", s_in, bv_at)
                p.wait("tensor", s_v, s_v.n)
                for dc in range(NC):
                    p.op("tensor", lambda e, dc=dc, b=b, t0=t0: e.matmul(
                        psA[:], wq_b[b][:, dc * 128:(dc + 1) * 128],
                        xn_fm[:, dc * T + t0: dc * T + t0 + W],
                        start=(dc == 0), stop=(dc == NC - 1)), s_t if dc == NC - 1 else None)
                qmm_at = s_t.n
                for dc in range(NC):
                    p.op("tensor", lambda e, dc=dc, b=b, t0=t0: e.matmul(
                        psB[:], wk_b[b][:, dc * 128:(dc + 1) * 128],
                        xn_fm[:, dc * T + t0: dc * T + t0 + W],
                        start=(dc == 0), stop=(dc == NC - 1)), s_t if dc == NC - 1 else None)
                kmm_at = s_t.n
                for ts in range(4):
                    for dc in range(NC):
                        p.op("tensor", lambda e, ts=ts, dc=dc, b=b, t0=t0: e.matmul(
                            psE[:, ts * 128:(ts + 1) * 128],
                            xn_fm[:, dc * T + t0 + ts * 128: dc * T + t0 + ts * 128 + 128],
                            wv_b[b][:, dc * 128:(dc + 1) * 128],
                            start=(ts == 0 and dc == 0), stop=False))
                    p.op("tensor", lambda e, ts=ts, hp=hp: e.matmul(
                        psE[:, ts * 128:(ts + 1) * 128], sel_sb[:, 0:128],
                        bv_sb[:, hp * 128:(hp + 1) * 128],
                        start=False, stop=(ts == 3)), s_t if ts == 3 else None)
                vmm_at = s_t.n

                p.wait("vector", s_t, qmm_at)
                p.op("vector", lambda e, hp=hp: e.tensor_scalar_add(
                    q_sb[:], psA[:], bqkv_sb[:, hp:hp + 1]), s_v)
                p.wait("vector", s_t, kmm_at)
                p.op("vector", lambda e, hp=hp: e.tensor_scalar_add(
                    k_sb[:], psB[:], bqkv_sb[:, 8 + hp:9 + hp]), s_v)
                p.wait("vector", s_t, vmm_at)
                p.op("vector", lambda e: e.tensor_copy(v_sb[:], psE[:]), s_v)
                wfree[b] = (s_v, s_v.n)
                qkv_at = s_v.n
                if "qkv3" in debug_parts and win == 3 and hp == 0:
                    p.wait("sync", s_v, qkv_at)
                    p.dma("sync", dbg["qkv3"][:, 0:W], q_sb[:], s_st)
                    p.wait("vector", s_st, s_st.n)
                    p.wait("tensor", s_st, s_st.n)
                    p.wait("scalar", s_st, s_st.n)
                    p.wait("gpsimd", s_st, s_st.n)
                    p.wait("sync", s_st, s_st.n)
                    p.dma("sync", dbg["qkv3"][:, W:2 * W], k_sb[:], s_st)
                    p.dma("sync", dbg["qkv3"][:, 2 * W:3 * W], v_sb[:], s_st)
                    p.wait("vector", s_st, s_st.n)
                    p.wait("tensor", s_st, s_st.n)
                    p.wait("scalar", s_st, s_st.n)
                    p.wait("gpsimd", s_st, s_st.n)
                    p.wait("sync", s_st, s_st.n)
                    p.wait("vector", s_st, s_st.n)
                    p.wait("tensor", s_st, s_st.n)
                    p.wait("sync", s_st, s_st.n)
                if "qkv" in debug_parts and win == 0 and hp == 0:
                    p.wait("sync", s_v, qkv_at)
                    p.dma("sync", dbg["qkv"][:, 0:W], q_sb[:], s_st)
                    p.wait("vector", s_st, s_st.n)
                    p.wait("tensor", s_st, s_st.n)
                    p.wait("scalar", s_st, s_st.n)
                    p.wait("gpsimd", s_st, s_st.n)
                    p.wait("sync", s_st, s_st.n)
                    p.wait("vector", s_st, s_st.n)
                    p.wait("tensor", s_st, s_st.n)
                    p.wait("sync", s_st, s_st.n)
                    p.dma("sync", dbg["qkv"][:, W:2 * W], k_sb[:], s_st)
                    p.wait("vector", s_st, s_st.n)
                    p.wait("tensor", s_st, s_st.n)
                    p.wait("scalar", s_st, s_st.n)
                    p.wait("gpsimd", s_st, s_st.n)
                    p.wait("sync", s_st, s_st.n)
                    p.dma("sync", dbg["qkv"][:, 2 * W:3 * W], v_sb[:], s_st)
                    p.wait("vector", s_st, s_st.n)
                    p.wait("tensor", s_st, s_st.n)
                    p.wait("sync", s_st, s_st.n)

                steps = [(hh, qs) for hh in range(2) for qs in range(4)]
                sc_done = [None] * 8
                sm_done = [None] * 8
                smread = [None] * 8
                tp_done2 = [None] * 8
                ev_done = [None] * 8

                def emit_score(s):
                    hh, qs = steps[s]
                    ps = psC if s % 2 == 0 else psD
                    if s < 2:
                        p.wait("tensor", s_v, qkv_at)
                    else:
                        p.wait("tensor", s_s, smread[s - 2])
                    p.op("tensor", lambda e, ps=ps, pofs=64 * hh, qs=qs: e.matmul(
                        ps[:], q_sb[pofs:pofs + 64, qs * 128:(qs + 1) * 128],
                        k_sb[pofs:pofs + 64, :], start=True, stop=True), s_t)
                    sc_done[s] = s_t.n

                def emit_softmax(s):
                    hh, qs = steps[s]
                    ps = psC if s % 2 == 0 else psD
                    pb = prob_b[s % 2]
                    if s >= 2:
                        p.wait("scalar", s_t, tp_done2[s - 2])
                    p.wait("scalar", s_t, sc_done[s])
                    p.op("scalar", lambda e, ps=ps, pb=pb: e.activation(
                        pb[:], ps[:], AF.Exp, scale=0.125), s_s)
                    smread[s] = s_s.n
                    p.wait("vector", s_s, s_s.n)
                    p.op("vector", lambda e, pb=pb: e.reduce_sum(stat[:, 2:3], pb[:], axis=AX))
                    vgap()
                    p.op("vector", lambda e: e.reciprocal(stat[:, 3:4], stat[:, 2:3]))
                    vgap()
                    p.op("vector", lambda e, pb=pb: e.tensor_scalar_mul(
                        pb[:], pb[:], stat[:, 3:4]), s_v)
                    sm_done[s] = s_v.n

                def emit_transposes(s):
                    hh, qs = steps[s]
                    pb = prob_b[s % 2]
                    p.wait("tensor", s_v, sm_done[s])
                    if s >= 2:
                        p.wait("tensor", s_v, ev_done[s - 2])
                    for ks in range(4):
                        p.op("tensor", lambda e, ks=ks, s=s, pb=pb: e.transpose(
                            psT[:, (s % 2) * 512 + ks * 128:(s % 2) * 512 + (ks + 1) * 128],
                            pb[:, ks * 128:(ks + 1) * 128], eye_sb[:]),
                            s_t if ks == 3 else None)
                    tp_done2[s] = s_t.n
                    p.wait("vector", s_t, tp_done2[s])
                    p.op("vector", lambda e, qs=qs, hh=hh, s=s: e.tensor_copy(
                        pt_b[hh][:].rearrange("p (ks q) -> p ks q", ks=4)[:, :, qs * 128:(qs + 1) * 128],
                        psT[:, (s % 2) * 512:(s % 2) * 512 + 512].rearrange(
                            "p (ks q) -> p ks q", ks=4)), s_v)
                    ev_done[s] = s_v.n

                def emit_av(hh):
                    p.wait("tensor", s_v, ev_done[hh * 4 + 3])
                    for ks in range(4):
                        p.op("tensor", lambda e, ks=ks, hh=hh: e.matmul(
                            psG[0:64, :], v_sb[:, ks * 128 + 64 * hh: ks * 128 + 64 * hh + 64],
                            pt_b[hh][:, ks * W:(ks + 1) * W],
                            start=(ks == 0), stop=(ks == 3)), s_t if ks == 3 else None)
                    p.wait("vector", s_t, s_t.n)
                    p.op("vector", lambda e, hh=hh: e.tensor_copy(
                        opre[:, hh * 512:(hh + 1) * 512], psG[0:64, :]), s_v)

                emit_score(0)
                emit_softmax(0)
                emit_score(1)
                for s in range(8):
                    if s + 1 < 8:
                        emit_softmax(s + 1)
                    if s + 2 < 8:
                        emit_score(s + 2)
                    emit_transposes(s)
                    if s == 3:
                        emit_av(0)
                    if s == 7:
                        emit_av(1)
                ow_at = s_v.n
                if "opre" in debug_parts and win == 0 and hp == 0:
                    p.wait("sync", s_v, ow_at)
                    p.dma("sync", dbg["opre"], opre[:], s_st)
                    p.wait("vector", s_st, s_st.n)
                    p.wait("tensor", s_st, s_st.n)
                    p.wait("scalar", s_st, s_st.n)
                    p.wait("gpsimd", s_st, s_st.n)
                    p.wait("sync", s_st, s_st.n)
                    p.wait("vector", s_st, s_st.n)
                    p.wait("tensor", s_st, s_st.n)
                    p.wait("sync", s_st, s_st.n)

                p.wait("tensor", wsem[0], wo_at)
                p.wait("tensor", s_v, ow_at)
                for cc in range(NC):
                    ps = psA if cc % 2 == 0 else psB
                    p.wait("tensor", s_v, s_v.n)
                    for hh in range(2):
                        p.op("tensor", lambda e, ps=ps, cc=cc, hh=hh: e.matmul(
                            ps[:], wo_sl0[:, hh * 1024 + cc * 128: hh * 1024 + (cc + 1) * 128],
                            opre[:, hh * 512:(hh + 1) * 512],
                            start=(hh == 0), stop=(hh == 1)), s_t if hh == 1 else None)
                    op_at = s_t.n
                    p.wait("vector", s_t, op_at)
                    if hp == 0:
                        p.op("vector", lambda e, ps=ps, cc=cc, win=win: e.tensor_scalar_add(
                            y_fm[:, cc * T + win * W: cc * T + (win + 1) * W], ps[:],
                            bout_sb[:, cc:cc + 1]), s_v)
                    else:
                        p.op("vector", lambda e, ps=ps, cc=cc, win=win: e.tensor_add(
                            y_fm[:, cc * T + win * W: cc * T + (win + 1) * W],
                            y_fm[:, cc * T + win * W: cc * T + (win + 1) * W], ps[:]), s_v)
                wofree[0] = (s_v, s_v.n)
        full_barrier()
        attn_done = s_v.n
        if "y_attn" in debug_parts:
            p.wait("sync", s_v, attn_done)
            p.dma("sync", dbg["y_attn"], y_fm[:], s_st)
            p.wait("vector", s_st, s_st.n)
            p.wait("tensor", s_st, s_st.n)
            p.wait("scalar", s_st, s_st.n)
            p.wait("gpsimd", s_st, s_st.n)
            p.wait("sync", s_st, s_st.n)
            p.wait("vector", s_st, s_st.n)
            p.wait("tensor", s_st, s_st.n)
            p.wait("sync", s_st, s_st.n)

        xb_free = [(s_v, attn_done), (s_v, attn_done)]
        obs = [ob0, ob1]
        ob_store_at = [None, None]
        for ti in range(NT):
            b = ti % 2
            if xb_free[b] is not None:
                p.wait("sync", *xb_free[b])
            p.dma("sync", xb[b][:], x_h[ti * 128:(ti + 1) * 128, :], s_in)
            xld_at = s_in.n
            p.wait("tensor", s_v, s_v.n)
            for cc in range(NC):
                p.op("tensor", lambda e, cc=cc, ti=ti: e.transpose(
                    psT[:, cc * 128:(cc + 1) * 128],
                    y_fm[:, cc * T + ti * 128: cc * T + ti * 128 + 128], eye_sb[:]),
                    s_t if cc == NC - 1 else None)
            ytp_at = s_t.n
            p.wait("vector", s_t, ytp_at)
            p.wait("vector", s_in, xld_at)
            if ob_store_at[b] is not None:
                p.wait("vector", s_st, ob_store_at[b])
            p.op("vector", lambda e, ob=obs[b], b=b: e.tensor_add(ob[:], xb[b][:], psT[:]), s_v)
            xb_free[b] = (s_v, s_v.n)
            add_at = s_v.n
            p.wait("sync", s_v, add_at)
            p.dma("sync", out_h[ti * 128:(ti + 1) * 128, :], obs[b][:], s_st)
            ob_store_at[b] = s_st.n
        partial_stores = s_st.n
        full_barrier()

        hid0_f = xn_fm[:, 0:8192]
        hid1_f = xn_fm[:, 8192:12288]
        xg0_fm = xn_fm[:, 12288:16384]
        xg1_fm = scr[:, 0:2048]
        eo0_fm = scr[:, 2048:6144]
        eo1_fm = scr[:, 6144:8192]
        xg_tm = [y_fm[:, 0:1024], y_fm[:, 1024:2048]]
        eo_tm = [y_fm[:, 2048:3072], y_fm[:, 3072:4096],
                 y_fm[:, 12288:13312], y_fm[:, 13312:14336],
                 y_fm[:, 14336:15360], y_fm[:, 15360:16384]]
        w12f = [y_fm[:, 4096:6144], y_fm[:, 6144:8192],
                scr[:, 8448:10496], scr[:, 10496:12544]]
        w3f = [y_fm[:, 8192:10240], y_fm[:, 10240:12288],
               scr[:, 12544:14592], scr[:, 14592:16640]]
        p.wait("sync", s_g, slot_scatters)
        p.dma("sync", slots_sb[:], slots_d.rearrange("(pp g) q -> pp (g q)", g=48), s_in)
        slots_rb = s_in.n
        p.wait("gpsimd", s_in, slots_rb)
        p.wait("gpsimd", s_st, partial_stores)
        w12i = 0
        w3i = 0
        w12free = [None, None, None, None]
        w3free = [None, None, None, None]
        xg_free = [None, None]
        expert_mark = None
        pending = []

        def flush_scatters():
            nonlocal_mark = expert_mark
            if nonlocal_mark is not None:
                p.wait("gpsimd", s_g, nonlocal_mark)
            for sti_, c_, sc_ in pending:
                p.wait("gpsimd", s_v, sc_)
                s_g.n += 16
                p.emit("gpsimd", lambda e, sti_=sti_, c_=c_: e.indirect_dma_start(
                    out_h, IOA(slots_sb[:, c_ * 4:c_ * 4 + 1], 0),
                    eo_tm[sti_][:], None, compute_op=ALU.add).then_inc(s_g.h, 16))
                if serial_dma:
                    p.wait("gpsimd", s_g, s_g.n)
            pending.clear()
            return s_g.n

        for e_ix in range(n_experts):
            cols = [e_ix * 6 + sti for sti in range(6)]

            def emit_gather(sti):
                b2 = sti % 2
                if xg_free[b2] is not None:
                    p.wait("gpsimd", s_t, xg_free[b2])
                s_g.n += 16
                p.emit("gpsimd", lambda e, c=cols[sti], b2=b2: e.indirect_dma_start(
                    xg_tm[b2][:], None, xn_d,
                    IOA(slots_sb[:, c * 4:c * 4 + 1], 0)).then_inc(s_g.h, 16))
                if serial_dma:
                    p.wait("gpsimd", s_g, s_g.n)
            for sti in range(2):
                emit_gather(sti)
            if pending:
                expert_mark = flush_scatters()
            for sti in range(6):
                b2 = sti % 2
                p.wait("tensor", s_g, s_g.n)
                p.wait("tensor", s_v, s_v.n)
                for cc in range(NC):
                    p.op("tensor", lambda e, cc=cc, b2=b2: e.transpose(
                        psT[:, cc * 128:(cc + 1) * 128],
                        xg_tm[b2][:, cc * 128:(cc + 1) * 128], eye_sb[:]),
                        s_t if cc == NC - 1 else None)
                xg_free[b2] = s_t.n
                if sti + 2 < 6:
                    emit_gather(sti + 2)
                p.wait("vector", s_t, s_t.n)
                if sti < 4:
                    p.op("vector", lambda e, sti=sti: e.tensor_copy(
                        xg0_fm.rearrange("p (dc t) -> p dc t", dc=NC)[
                            :, :, sti * 128:(sti + 1) * 128],
                        psT[:].rearrange("p (cc t) -> p cc t", cc=NC)), s_v)
                else:
                    p.op("vector", lambda e, sti=sti: e.tensor_copy(
                        xg1_fm.rearrange("p (dc t) -> p dc t", dc=NC)[
                            :, :, (sti - 4) * 128:(sti - 3) * 128],
                        psT[:].rearrange("p (cc t) -> p cc t", cc=NC)), s_v)
            xg_at = s_v.n
            bank_free = {}
            for hc in range(NHC):
                b = w12i % 4
                w12i += 1
                if w12free[b] is not None:
                    p.wait("sync", *w12free[b])
                    p.wait("gpsimd", *w12free[b])
                p.dma("sync", w12f[b][:, 0:C], w12t_h[e_ix, hc][:, 0:C], wsem[0])
                p.dma("gpsimd", w12f[b][:, C:2 * C], w12t_h[e_ix, hc][:, C:2 * C],
                      asem[0])
                w12_at = wsem[0].n
                w12a_at = asem[0].n
                ps2 = psA if hc % 2 == 0 else psB
                ps1 = psB if hc % 2 == 0 else psA
                ps2g = psC if hc % 2 == 0 else psD
                ps1g = psD if hc % 2 == 0 else psC
                p.wait("tensor", wsem[0], w12_at)
                p.wait("tensor", asem[0], w12a_at)
                if hc == 0:
                    p.wait("tensor", s_v, s_v.n)
                else:
                    if bank_free.get(id(ps2)) is not None:
                        p.wait("tensor", *bank_free[id(ps2)])
                    if bank_free.get(id(ps2g)) is not None:
                        p.wait("tensor", *bank_free[id(ps2g)])
                for dc in range(NC):
                    p.op("tensor", lambda e, dc=dc, b=b, ps2=ps2: e.matmul(
                        ps2[:, 0:512], w12f[b][:, C + dc * 128: C + (dc + 1) * 128],
                        xg0_fm[:, dc * 512:(dc + 1) * 512],
                        start=(dc == 0), stop=(dc == NC - 1)), s_t if dc == NC - 1 else None)
                for dc in range(NC):
                    p.op("tensor", lambda e, dc=dc, b=b, ps2g=ps2g: e.matmul(
                        ps2g[:, 0:256], w12f[b][:, C + dc * 128: C + (dc + 1) * 128],
                        xg1_fm[:, dc * 256:(dc + 1) * 256],
                        start=(dc == 0), stop=(dc == NC - 1)), s_t if dc == NC - 1 else None)
                h2_at = s_t.n
                if hc > 0:
                    if bank_free.get(id(ps1)) is not None:
                        p.wait("tensor", *bank_free[id(ps1)])
                    if bank_free.get(id(ps1g)) is not None:
                        p.wait("tensor", *bank_free[id(ps1g)])
                for dc in range(NC):
                    p.op("tensor", lambda e, dc=dc, b=b, ps1=ps1: e.matmul(
                        ps1[:, 0:512], w12f[b][:, dc * 128:(dc + 1) * 128],
                        xg0_fm[:, dc * 512:(dc + 1) * 512],
                        start=(dc == 0), stop=(dc == NC - 1)), s_t if dc == NC - 1 else None)
                h1g0_at = s_t.n
                for dc in range(NC):
                    p.op("tensor", lambda e, dc=dc, b=b, ps1g=ps1g: e.matmul(
                        ps1g[:, 0:256], w12f[b][:, dc * 128:(dc + 1) * 128],
                        xg1_fm[:, dc * 256:(dc + 1) * 256],
                        start=(dc == 0), stop=(dc == NC - 1)), s_t if dc == NC - 1 else None)
                h1_at = s_t.n
                p.wait("vector", s_t, h2_at)
                p.op("vector", lambda e, ps2=ps2, hc=hc: e.tensor_copy(
                    hid0_f[:, hc * 512:(hc + 1) * 512], ps2[:, 0:512]), s_v)
                p.op("vector", lambda e, ps2g=ps2g, hc=hc: e.tensor_copy(
                    hid1_f[:, hc * 256:(hc + 1) * 256], ps2g[:, 0:256]), s_v)
                bank_free[id(ps2)] = (s_v, s_v.n)
                bank_free[id(ps2g)] = (s_v, s_v.n)
                to = (hc % 2) * 512
                tg = (hc % 2) * 256
                if bank_free.get(("psTsilu", hc % 2)) is not None:
                    p.wait("scalar", *bank_free[("psTsilu", hc % 2)])
                p.wait("scalar", s_t, h1g0_at)
                p.op("scalar", lambda e, ps1=ps1, to=to: e.activation(
                    psT[:, to:to + 512], ps1[:, 0:512], AF.Silu), s_s)
                p.wait("scalar", s_t, h1_at)
                p.op("scalar", lambda e, ps1g=ps1g, tg=tg: e.activation(
                    psG[:, tg:tg + 256], ps1g[:, 0:256], AF.Silu), s_s)
                bank_free[id(ps1)] = (s_s, s_s.n)
                bank_free[id(ps1g)] = (s_s, s_s.n)
                p.wait("vector", s_s, s_s.n)
                p.op("vector", lambda e, hc=hc, to=to: e.tensor_mul(
                    hid0_f[:, hc * 512:(hc + 1) * 512], hid0_f[:, hc * 512:(hc + 1) * 512],
                    psT[:, to:to + 512]), s_v)
                p.op("vector", lambda e, hc=hc, tg=tg: e.tensor_mul(
                    hid1_f[:, hc * 256:(hc + 1) * 256], hid1_f[:, hc * 256:(hc + 1) * 256],
                    psG[:, tg:tg + 256]), s_v)
                bank_free[("psTsilu", hc % 2)] = (s_v, s_v.n)
                w12free[b] = (s_v, s_v.n)
            for dd in range(NC):
                b = w3i % 4
                w3i += 1
                if w3free[b] is not None:
                    p.wait("sync", *w3free[b])
                    p.wait("gpsimd", *w3free[b])
                p.dma("sync", w3f[b][:, 0:1024], w3t_h[e_ix, dd][:, 0:1024], wsem[0])
                p.dma("gpsimd", w3f[b][:, 1024:2048], w3t_h[e_ix, dd][:, 1024:2048],
                      asem[0])
                w3_at = wsem[0].n
                w3a_at = asem[0].n
                ps = psC if dd % 2 == 0 else psD
                psg = psE if dd % 2 == 0 else psG
                p.wait("tensor", wsem[0], w3_at)
                p.wait("tensor", asem[0], w3a_at)
                p.wait("tensor", s_v, s_v.n)
                for hc in range(NHC):
                    p.op("tensor", lambda e, hc=hc, b=b, ps=ps: e.matmul(
                        ps[:, 0:512], w3f[b][:, hc * 128:(hc + 1) * 128],
                        hid0_f[:, hc * 512:(hc + 1) * 512],
                        start=(hc == 0), stop=(hc == NHC - 1)), s_t if hc == NHC - 1 else None)
                for hc in range(NHC):
                    p.op("tensor", lambda e, hc=hc, b=b, psg=psg: e.matmul(
                        psg[:, 0:256], w3f[b][:, hc * 128:(hc + 1) * 128],
                        hid1_f[:, hc * 256:(hc + 1) * 256],
                        start=(hc == 0), stop=(hc == NHC - 1)), s_t if hc == NHC - 1 else None)
                w3mm_at = s_t.n
                p.wait("vector", s_t, w3mm_at)
                p.op("vector", lambda e, ps=ps, dd=dd: e.tensor_copy(
                    eo0_fm[:, dd * 512:(dd + 1) * 512], ps[:, 0:512]), s_v)
                p.op("vector", lambda e, psg=psg, dd=dd: e.tensor_copy(
                    eo1_fm[:, dd * 256:(dd + 1) * 256], psg[:, 0:256]), s_v)
                w3free[b] = (s_v, s_v.n)
            for sti in range(6):
                b2 = sti % 2
                p.wait("tensor", s_v, s_v.n)
                for dd in range(NC):
                    if sti < 4:
                        p.op("tensor", lambda e, dd=dd, sti=sti: e.transpose(
                            psT[:, dd * 128:(dd + 1) * 128],
                            eo0_fm[:, dd * 512 + sti * 128: dd * 512 + sti * 128 + 128],
                            eye_sb[:]), s_t if dd == NC - 1 else None)
                    else:
                        p.op("tensor", lambda e, dd=dd, sti=sti: e.transpose(
                            psT[:, dd * 128:(dd + 1) * 128],
                            eo1_fm[:, dd * 256 + (sti - 4) * 128:
                                   dd * 256 + (sti - 4) * 128 + 128],
                            eye_sb[:]), s_t if dd == NC - 1 else None)
                eot_at = s_t.n
                p.wait("vector", s_t, eot_at)
                if e_ix > 0:
                    p.wait("vector", s_g, s_g.n)
                p.op("vector", lambda e, sti=sti, c=cols[sti]: e.tensor_scalar_mul(
                    eo_tm[sti][:].rearrange("p (dd t) -> p dd t", dd=NC),
                    psT[:].rearrange("p (dd t) -> p dd t", dd=NC),
                    slots_sb[:, c * 4 + 1:c * 4 + 2].bitcast(F32)), s_v)
                sc_at = s_v.n
                if do_scatter:
                    pending.append((sti, cols[sti], sc_at))
        if pending:
            expert_mark = flush_scatters()
        p.wait("sync", s_st, s_st.n)
        p.wait("sync", s_g, s_g.n)

        @block.sync
        def _(eng):
            for f in p.q["sync"]:
                f(eng)

        @block.tensor
        def _(eng):
            for f in p.q["tensor"]:
                f(eng)

        @block.vector
        def _(eng):
            for f in p.q["vector"]:
                f(eng)

        @block.scalar
        def _(eng):
            for f in p.q["scalar"]:
                f(eng)

        @block.gpsimd
        def _(eng):
            for f in p.q["gpsimd"]:
                f(eng)

    return nc


_RAMP = np.tile(np.arange(8, 0, -1, dtype=np.float32), (128, 1))
_EYE = np.eye(128, dtype=np.float32)
_SEL = np.zeros((8, 8 * 128), dtype=np.float32)
for _e in range(8):
    _SEL[_e, _e * 128:(_e + 1) * 128] = 1.0
_TVALS = (np.arange(16)[None, :] * 128 + np.arange(128)[:, None]).astype(np.int32)
_SUT = np.triu(np.ones((128, 128), dtype=np.float32), k=1)
_ONESM = np.ones((128, 128), dtype=np.float32)


def kernel(L0_attn_x, L0_attn_w_qkv, L0_attn_b_qkv, L0_attn_w_out, L0_attn_b_out,
           L0_ffn_w12, L0_ffn_w3, L0_ffn_gate_w):
    x = np.ascontiguousarray(np.asarray(L0_attn_x, dtype=np.float32))
    wqkv = np.ascontiguousarray(np.asarray(L0_attn_w_qkv, dtype=np.float32))
    bqkv = np.ascontiguousarray(np.asarray(L0_attn_b_qkv, dtype=np.float32))
    wout = np.ascontiguousarray(np.asarray(L0_attn_w_out, dtype=np.float32))
    bout = np.ascontiguousarray(np.asarray(L0_attn_b_out, dtype=np.float32))
    w12t = np.asarray(L0_ffn_w12, dtype=np.float32).transpose(0, 2, 1)
    w3t = np.asarray(L0_ffn_w3, dtype=np.float32).transpose(0, 2, 1)
    w12t = np.ascontiguousarray(
        w12t.reshape(E, NC, 128, 2, NHC, 128).transpose(0, 4, 2, 3, 1, 5).reshape(
            E, NHC, 128, 2 * C))
    w3t = np.ascontiguousarray(
        w3t.reshape(E, NHC, 128, NC, 128).transpose(0, 3, 2, 1, 4).reshape(
            E, NC, 128, NHC * 128))
    gwt = np.ascontiguousarray(np.asarray(L0_ffn_gate_w, dtype=np.float32).T)
    bqkv_r = np.ascontiguousarray(bqkv.reshape(24, 128).T)
    bvrow = np.zeros((8, 1024), dtype=np.float32)
    bvrow[0] = bqkv[2048:3072]
    bout_r = np.ascontiguousarray(bout.reshape(8, 128).T)

    nc = build_program(debug=False)
    shared = {"wqkv": wqkv, "bqkv": bqkv_r, "bvrow": bvrow, "wout": wout, "bout": bout_r,
              "w12t": w12t, "w3t": w3t, "gwt": gwt,
              "eye": _EYE, "ramp": _RAMP, "sel": _SEL,
              "tvals": _TVALS, "sut": _SUT, "onesm": _ONESM}
    in_maps = [dict(shared, x=np.ascontiguousarray(x[i])) for i in range(8)]
    res = run_bass_kernel_spmd(nc, in_maps, core_ids=list(range(8)))
    out = np.stack([res.results[i]["out"] for i in range(8)], axis=0)
    return out.astype(np.float32)
